# version 21
# baseline (speedup 1.0000x reference)
import math
import numpy as np
import ml_dtypes
import concourse.bass as bass
import concourse.mybir as mybir
from concourse.bass_utils import run_bass_kernel_spmd

F32 = mybir.dt.float32
BF16 = mybir.dt.bfloat16
U8 = mybir.dt.uint8
AF = mybir.ActivationFunctionType
ALU = mybir.AluOpType
AX = mybir.AxisListType

B, S, D = 16, 2048, 1024
NCORE = 8
BLOC = B // NCORE
NH, DH, NPAIR = 16, 64, 8
NT = S // 128
H = 1408
NMO = H // 128
NE = 9
NKI = D // 128
HALF = 1024
NTL = HALF // 128
LN_EPS = 1e-5
RMS_EPS = float(np.finfo(np.float32).eps)

DEBUG = False
ENGS = ("sync", "gpsimd", "tensor", "vector", "scalar")


class Sem:
    __slots__ = ("h", "n")

    def __init__(self, h):
        self.h = h
        self.n = 0


class Prog:
    def __init__(self):
        self.ops = {e: [] for e in ENGS}

    def emit(self, eng, meth, *args, wait=None, incs=None, **kw):
        self.ops[eng].append((meth, args, kw, list(wait or []), list(incs or [])))

    def replay(self, eng_obj, name):
        for meth, args, kw, waits, incs in self.ops[name]:
            for s, v in waits:
                if v > 0:
                    eng_obj.wait_ge(s.h, v)
            r = getattr(eng_obj, meth)(*args, **kw)
            for s, k in incs:
                r.then_inc(s.h, k)


def _rope_tables():
    inv = (1.0 / (10000.0 ** (np.arange(0, DH, 2, dtype=np.float32) / DH))).astype(np.float32)
    t = np.arange(S, dtype=np.float32)
    ang = (inv[:, None] * t[None, :]).astype(np.float32)
    cos = np.cos(ang).astype(np.float32)
    sin = np.sin(ang).astype(np.float32)
    cosT = np.concatenate([cos, cos], 0)
    sinT = np.concatenate([sin, sin], 0)
    return np.tile(cosT, (2, 1)), np.tile(sinT, (2, 1))


COS128, SIN128 = _rope_tables()
ID_BF = np.eye(128, dtype=ml_dtypes.bfloat16)
PROT = np.zeros((128, 128), dtype=ml_dtypes.bfloat16)
for _i in range(64):
    PROT[2 * _i + 1, 2 * _i] = -1.0
    PROT[2 * _i, 2 * _i + 1] = 1.0
TRI = np.triu(np.ones((128, 128), dtype=np.float32)).astype(ml_dtypes.bfloat16)
ONES2 = np.zeros((128, 2), dtype=ml_dtypes.bfloat16)
ONES2[0:64, 0] = 1.0
ONES2[64:128, 1] = 1.0
SEL2 = np.zeros((2, 128), dtype=ml_dtypes.bfloat16)
SEL2[0, 0:64] = 1.0
SEL2[1, 64:128] = 1.0
ONESROW = np.ones((1, 128), dtype=ml_dtypes.bfloat16)
CONSTVALS = [LN_EPS, RMS_EPS, -math.log(DH), 1.0 / DH, -0.5, 1.0 / 1024.0, -1e30]
CONSTNP = np.tile(np.array([CONSTVALS], dtype=np.float32), (128, 1))


def build_nc():
    nc = bass.Bass("TRN2", target_bir_lowering=False, detect_race_conditions=False)

    x_h = nc.dram_tensor("x", [BLOC, S, D], F32, kind="ExternalInput").ap()
    wqt_h = nc.dram_tensor("wqt", [128, NKI, D], F32, kind="ExternalInput").ap()
    wkt_h = nc.dram_tensor("wkt", [128, NKI, D], F32, kind="ExternalInput").ap()
    wvt_h = nc.dram_tensor("wvt", [128, NKI, D], F32, kind="ExternalInput").ap()
    wot_h = nc.dram_tensor("wot", [128, NKI, D], F32, kind="ExternalInput").ap()
    bqt_h = nc.dram_tensor("bqt", [128, NPAIR], F32, kind="ExternalInput").ap()
    bkt_h = nc.dram_tensor("bkt", [128, NPAIR], F32, kind="ExternalInput").ap()
    bvr_h = nc.dram_tensor("bvr", [1, D], F32, kind="ExternalInput").ap()
    nw_h = nc.dram_tensor("nw", [128, 1], F32, kind="ExternalInput").ap()
    nws_h = nc.dram_tensor("nws", [128, 1], F32, kind="ExternalInput").ap()
    gwt_h = nc.dram_tensor("gwt", [128, NKI * 8], F32, kind="ExternalInput").ap()
    gb_h = nc.dram_tensor("gb", [1, 8], F32, kind="ExternalInput").ap()
    b1t_h = nc.dram_tensor("b1t", [128, NE * NMO], F32, kind="ExternalInput").ap()
    b3t_h = nc.dram_tensor("b3t", [128, NE * NMO], F32, kind="ExternalInput").ap()
    bp_h = nc.dram_tensor("bp", [NE, D], F32, kind="ExternalInput").ap()
    w1_h = nc.dram_tensor("w1", [8, 128, NKI, H], F32, kind="ExternalInput").ap()
    w3_h = nc.dram_tensor("w3", [8, 128, NKI, H], F32, kind="ExternalInput").ap()
    w2_h = nc.dram_tensor("w2", [8, 128, NMO, D], F32, kind="ExternalInput").ap()
    sw1_h = nc.dram_tensor("sw1", [128, NKI, H], F32, kind="ExternalInput").ap()
    sw3_h = nc.dram_tensor("sw3", [128, NKI, H], F32, kind="ExternalInput").ap()
    sw2_h = nc.dram_tensor("sw2", [128, NMO, D], F32, kind="ExternalInput").ap()
    out_h = nc.dram_tensor("out", [BLOC, S, D], F32, kind="ExternalOutput").ap()
    dbg = {}
    if DEBUG:
        for nm, shp, dt in [("d_cw", [128, S], BF16), ("d_sw", [128, S], F32),
                            ("d_statm", [128, NT], F32), ("d_statr", [128, NT], F32),
                            ("d_xnT", [128, NKI * S], BF16), ("d_zT", [128, NPAIR * S], BF16),
                            ("d_rw", [128, NT * 8], F32), ("d_rwT", [9, NT * 128], BF16),
                            ("d_u1", [128, 32768], U8),
                            ("d_y", [128, 8192], F32), ("d_gate", [128, NKI * 8], BF16),
                            ("d_qT", [128, S], BF16), ("d_kT", [128, S], BF16),
                            ("d_v", [128, 16384], BF16)]:
            dbg[nm] = nc.dram_tensor(nm, shp, dt, kind="ExternalOutput").ap()

    cos_c = nc.inline_tensor(COS128, "cos128").ap()
    sin_c = nc.inline_tensor(SIN128, "sin128").ap()
    idbf_c = nc.inline_tensor(ID_BF, "idbf").ap()
    prot_c = nc.inline_tensor(PROT, "prot").ap()
    tri_c = nc.inline_tensor(TRI, "tri").ap()
    ones2_c = nc.inline_tensor(ONES2, "ones2").ap()
    sel2_c = nc.inline_tensor(SEL2, "sel2").ap()
    onesrow_c = nc.inline_tensor(ONESROW, "onesrow").ap()
    cvals_c = nc.inline_tensor(CONSTNP, "cvals").ap()

    w1s_h = nc.dram_tensor("w1s", [NE, 128, NMO * 1024], BF16, kind="Internal").ap()
    w3s_h = nc.dram_tensor("w3s", [NE, 128, NMO * 1024], BF16, kind="Internal").ap()
    w2s_h = nc.dram_tensor("w2s", [NE, 128, NMO * 1024], BF16, kind="Internal").ap()
    wos_h = nc.dram_tensor("wos", [128, NKI * 1024], BF16, kind="Internal").ap()
    wqs_h = nc.dram_tensor("wqs", [128, NKI, 1024], BF16, kind="Internal").ap()
    wks_h = nc.dram_tensor("wks", [128, NKI, 1024], BF16, kind="Internal").ap()
    wvs_h = nc.dram_tensor("wvs", [128, NKI, 1024], BF16, kind="Internal").ap()

    def A(nm, shp, dt=F32):
        return nc.alloc_sbuf_tensor(nm, shp, dt).ap()

    xnT = A("xnT", [128, NKI * S], BF16)
    zT = A("zT", [128, NPAIR * S], BF16)
    u1 = nc.alloc_sbuf_tensor("u1", [128, 32768], U8)
    v_nat = u1.ap().bitcast(BF16)
    y_acc = u1.ap().bitcast(F32)
    u2 = nc.alloc_sbuf_tensor("u2", [128, 59392], U8)
    u2b = u2.ap().bitcast(BF16)
    u2f = u2.ap().bitcast(F32)

    def cb(off, n):
        assert off % 2 == 0
        return u2b[:, off // 2: off // 2 + n]

    def cf(off, n):
        assert off % 4 == 0
        return u2f[:, off // 4: off // 4 + n]

    hT = cb(0, NMO * 1024)
    w2sb = cb(22528, NMO * 1024)
    w1m = [cb(45056 + i * 2048, 1024) for i in range(2)]
    w3m = [cb(49152 + i * 2048, 1024) for i in range(2)]
    t3b = cf(53248, 512)
    prodb = [cf(55296 + i * 2048, 512) for i in range(2)]
    qTp = cb(0, S)
    kTp = cb(4096, S)
    wqp2 = [cb(8192 + i * 2048, NKI * 128) for i in range(2)]
    wvsb = cb(12288, NKI * 1024)
    wkp2 = [cb(12288 + i * 2048, NKI * 128) for i in range(2)]
    pbuf = [cb(28672 + i * 1024, 512) for i in range(4)]
    q1b = cf(32768, 512)
    t1buf = cb(34816, 512)
    t2buf = cb(35840, 512)
    sqb = cb(36864, 512)
    rstd_f = cf(37888, 512)[0:2, :]
    rstd_b = cb(39936, 512)[0:2, :]
    cosst = cf(0, 2048)
    bvr_sb = cb(40960, 1024)[0:1, :]
    q1b_k = cf(16384, 512)
    sqb_k = cb(18432, 512)
    t1buf_k = cb(19456, 512)
    t2buf_k = cb(20480, 512)
    rstd_f_k = cf(21504, 512)[0:2, :]
    rstd_b_k = cb(23552, 512)[0:2, :]
    wosb = cb(0, NKI * 1024)

    wstage = [A(f"wstage{i}", [128, 1024], BF16) for i in range(2)]
    bvr_st = A("bvr_st", [1, D], BF16)
    cw = A("cw", [128, S], BF16)
    sw = A("sw", [128, S], F32)
    xt = [A(f"xt{i}", [128, D], F32) for i in range(2)]
    outb = [A(f"outb{i}", [128, D], F32) for i in range(2)]
    gbuf = A("gbuf", [128, D], F32)
    sbuf_s = A("sbufs", [128, D], F32)
    stat_m = A("stat_m", [128, NT], F32)
    stat_r = A("stat_r", [128, NT], F32)
    rw_sb = A("rw_sb", [128, NT * 8], F32)
    rwT = A("rwT", [9, NT * 128], BF16)
    rwx = A("rwx", [128, 16], BF16)
    gate_sb = A("gate_sb", [128, NKI * 8], BF16)
    gateb_b = A("gateb_b", [128, 8], F32)
    gb_sb = A("gb_sb", [1, 8], BF16)
    bqt_sb = A("bqt_sb", [128, NPAIR], F32)
    bkt_sb = A("bkt_sb", [128, NPAIR], F32)
    nw_sb = A("nw_sb", [128, 1], F32)
    nws_sb = A("nws_sb", [128, 1], F32)
    b1t_sb = A("b1t_sb", [128, NE * NMO], F32)
    b3t_sb = A("b3t_sb", [128, NE * NMO], F32)
    bp_sb = A("bp_sb", [NE, D], BF16)
    idbf_sb = A("idbf_sb", [128, 128], BF16)
    prot_sb = A("prot_sb", [128, 128], BF16)
    tri_sb = A("tri_sb", [128, 128], BF16)
    ones2_sb = A("ones2_sb", [128, 2], BF16)
    sel2_sb = A("sel2_sb", [2, 128], BF16)
    onesrow_sb = A("onesrow_sb", [1, 128], BF16)
    tiny = A("tiny", [128, 8], F32)
    tiny2 = A("tiny2", [128, 8], F32)
    tiny3 = A("tiny3", [128, 8], F32)
    cst_sb = A("cst_sb", [128, len(CONSTVALS)], F32)
    for _ci, _cv in enumerate(CONSTVALS):
        nc.const_aps.register(F32, float(_cv), cst_sb[:, _ci:_ci + 1])
    m1c = A("m1c", [128, 1], F32)
    m2c = A("m2c", [128, 1], F32)
    s1c = A("s1c", [128, 1], F32)
    s2c = A("s2c", [128, 1], F32)

    pb = [nc.alloc_psum_tensor(f"pb{i}", [128, 512], F32).ap() for i in range(6)]
    pt = [nc.alloc_psum_tensor(f"pt{i}", [128, 1024], BF16).ap() for i in range(2)]

    PE = Sem(nc.alloc_semaphore("PE"))
    DV = Sem(nc.alloc_semaphore("DV"))
    AC = Sem(nc.alloc_semaphore("AC"))
    BS = {e: Sem(nc.alloc_semaphore(f"BS_{e}")) for e in ENGS}

    def inc(s, k=1):
        s.n += k
        return (s, k)

    p = Prog()
    E = p.emit

    class Dq:

        def __init__(self, name, eng, nslots):
            self.eng = eng
            self.slots = [Sem(nc.alloc_semaphore(f"{name}{i}")) for i in range(nslots)]
            self.i = 0

        def dma(self, out, in_, wait=None, **kw):
            s = self.slots[self.i % len(self.slots)]
            self.i += 1
            w = list(wait or [])
            if s.n:
                w.append((s, s.n))
            s.n += 16
            E(self.eng, "dma_start", out=out, in_=in_, wait=w,
              incs=[(s, 16)], **kw)
            return (s, s.n)

        def drain_tokens(self):
            return [(s, s.n) for s in self.slots if s.n]

    QX = Dq("QX", "sync", 5)
    QS = Dq("QS", "sync", 6)
    QW = Dq("QW", "gpsimd", 6)
    QO = Dq("QO", "sync", 3)

    def wmax(tokens):
        best = {}
        for s, v in tokens:
            if v > best.get(id(s), (None, -1))[1]:
                best[id(s)] = (s, v)
        return list(best.values())

    BENGS = ("sync", "tensor", "vector", "scalar")

    def barrier():
        for eng in BENGS:
            E(eng, "nop", incs=[inc(BS[eng], 1)])
        vals = {eng: BS[eng].n for eng in BENGS}
        for eng in BENGS:
            E(eng, "nop", wait=[(BS[o], vals[o]) for o in BENGS if o != eng])

    setup_toks = []
    for dst, srcap in [(bqt_sb, bqt_h), (bkt_sb, bkt_h), (nw_sb, nw_h),
                       (nws_sb, nws_h), (b1t_sb, b1t_h), (b3t_sb, b3t_h),
                       (idbf_sb, idbf_c), (prot_sb, prot_c), (tri_sb, tri_c),
                       (ones2_sb, ones2_c), (sel2_sb, sel2_c),
                       (onesrow_sb, onesrow_c), (cst_sb, cvals_c)]:
        setup_toks.append(QX.dma(dst, srcap))
    setup_toks.append(QW.dma(bp_sb, bp_h))
    setup_toks.append(QW.dma(gate_sb, gwt_h))
    setup_toks.append(QW.dma(gb_sb, gb_h))
    setup_toks.append(QW.dma(bvr_st, bvr_h))
    ctok = QX.dma(cosst, cos_c)
    E("vector", "tensor_scalar_mul", cw, cosst, nw_sb,
      wait=wmax(setup_toks + [ctok]), incs=[inc(DV)])
    ctok = QX.dma(cosst, sin_c, wait=[(DV, DV.n)])
    E("vector", "tensor_scalar_mul", sw, cosst, nws_sb, wait=[ctok],
      incs=[inc(DV)])
    _wsi = [0]

    def _cast_slab(dst, srcap):
        i = _wsi[0] % 2
        _wsi[0] += 1
        t = QW.dma(wstage[i], srcap)
        return QW.dma(dst, wstage[i], wait=[t])

    wv_cast = wmax([_cast_slab(wvs_h[:, ki, :], wvt_h[:, ki, :])
                    for ki in range(NKI)])
    wq_cast = wmax([_cast_slab(wqs_h[:, ki, :], wqt_h[:, ki, :])
                    for ki in range(NKI)])
    wk_cast = wmax([_cast_slab(wks_h[:, ki, :], wkt_h[:, ki, :])
                    for ki in range(NKI)])
    wo_cast = wmax([_cast_slab(wos_h[:, ki * 1024:(ki + 1) * 1024], wot_h[:, ki, :])
                    for ki in range(NKI)])
    wcast_tok = {}
    for e in range(NE):
        toks = []
        for mo in range(NMO):
            toks.append(_cast_slab(
                w1s_h[e, :, mo * 1024:(mo + 1) * 1024],
                (w1_h[e, :, :, mo * 128:(mo + 1) * 128] if e < 8
                 else sw1_h[:, :, mo * 128:(mo + 1) * 128])))
            toks.append(_cast_slab(
                w3s_h[e, :, mo * 1024:(mo + 1) * 1024],
                (w3_h[e, :, :, mo * 128:(mo + 1) * 128] if e < 8
                 else sw3_h[:, :, mo * 128:(mo + 1) * 128])))
        for kh in range(NMO):
            toks.append(_cast_slab(
                w2s_h[e, :, kh * 1024:(kh + 1) * 1024],
                w2_h[e, :, kh, :] if e < 8 else sw2_h[:, kh, :]))
        wcast_tok[e] = wmax(toks)
    E("tensor", "matmul", pb[0][:, 0:8], onesrow_sb, gb_sb, start=True, stop=True,
      wait=wmax(setup_toks), incs=[inc(PE)])
    E("vector", "tensor_copy", gateb_b, pb[0][:, 0:8], wait=[(PE, PE.n)],
      incs=[inc(DV)])
    barrier()

    slab_g = {0: 0, 1: 0}
    h_dv = {0: 0, 1: 0}
    h_ac = {0: 0, 1: 0}
    oe_dv = {0: 0, 1: 0}
    oe_last_pe = 0

    xt_guard = {0: 0, 1: 0}
    for seq in range(BLOC):
        xa_dv = {}
        pt_ac = {0: 0, 1: 0}
        tp_pe_prev = 0
        xnbf = sbuf_s.bitcast(BF16)[:, 0:1024]
        xnT_ac = {}
        vmark = {}
        v_dv = {0: 0, 1: 0}
        gate_ac = {0: 0, 1: 0}
        gate_dv = {0: 0, 1: 0}
        wv_toks = [QS.dma(wvsb[:, ki * 1024:(ki + 1) * 1024], wvs_h[:, ki, :],
                          wait=(wv_cast + [(PE, PE.n)] if seq == 0 and ki == 0
                                else ([(PE, PE.n)] if ki == 0 else [])))
                   for ki in range(NKI)]
        wv_done = wmax(wv_toks)
        for tau in range(NT):
            b = tau % 2
            xload = QX.dma(xt[b], x_h[seq, tau * 128:(tau + 1) * 128, :],
                           wait=([(DV, xt_guard[b])] if xt_guard[b] else []))
            E("scalar", "activation", gbuf, xt[b], AF.Square,
              wait=[xload], incs=[inc(AC)])
            sq_done = AC.n
            E("vector", "reduce_sum", s1c, xt[b], axis=AX.X, wait=[xload])
            E("vector", "reduce_sum", s2c, gbuf, axis=AX.X, wait=[(AC, sq_done)])
            E("vector", "tensor_scalar_mul", stat_m[:, tau:tau + 1], s1c, 1.0 / D,
              incs=[inc(DV)])
            E("vector", "tensor_tensor", m1c, stat_m[:, tau:tau + 1],
              stat_m[:, tau:tau + 1], ALU.mult, wait=[(DV, DV.n)], incs=[inc(DV)])
            E("vector", "scalar_tensor_tensor", m2c, s2c, 1.0 / D, m1c,
              ALU.mult, ALU.subtract, wait=[(DV, DV.n)], incs=[inc(DV)])
            var_done = DV.n
            E("scalar", "activation", m2c, m2c, AF.Ln, bias=LN_EPS,
              wait=[(DV, var_done)])
            E("scalar", "activation", stat_r[:, tau:tau + 1], m2c, AF.Exp,
              scale=-0.5, incs=[inc(AC)])
            rstd_done = AC.n
            E("vector", "tensor_scalar", xnbf, xt[b], stat_m[:, tau:tau + 1],
              stat_r[:, tau:tau + 1], ALU.subtract, ALU.mult,
              wait=[(AC, rstd_done), (PE, tp_pe_prev)], incs=[inc(DV)])
            xa_dv[tau] = DV.n
            xt_guard[b] = DV.n
            xn_done = DV.n
            ptb = pt[tau % 2]
            for ki in range(NKI):
                E("tensor", "transpose", ptb[:, ki * 128:(ki + 1) * 128],
                  xnbf[:, ki * 128:(ki + 1) * 128], idbf_sb,
                  wait=([(DV, xn_done), (AC, pt_ac[tau % 2])] if ki == 0 else []),
                  incs=([inc(PE)] if ki == NKI - 1 else []))
            tp_pe_prev = PE.n
            for ki in range(NKI):
                w = []
                if ki == 0:
                    w = [(PE, tp_pe_prev)]
                    if seq > 0 and tau == 0:
                        w.append((PE, PE.n))
                E("scalar", "copy",
                  xnT[:, ki * S + tau * 128: ki * S + (tau + 1) * 128],
                  ptb[:, ki * 128:(ki + 1) * 128],
                  wait=w, incs=([inc(AC)] if ki == NKI - 1 else []))
            pt_ac[tau % 2] = AC.n
            xnT_ac[tau] = AC.n
        for tau in range(NT):
            b = tau % 2
            bank = pb[2 + tau % 2]
            for ki in range(NKI):
                E("tensor", "matmul", bank[:, 0:8],
                  xnT[:, ki * S + tau * 128: ki * S + (tau + 1) * 128],
                  gate_sb[:, ki * 8:(ki + 1) * 8],
                  start=(ki == 0), stop=(ki == NKI - 1),
                  wait=([(AC, max(xnT_ac[tau], gate_ac[tau % 2])),
                         (DV, gate_dv[tau % 2])] if ki == 0 else []),
                  incs=([inc(PE)] if ki == NKI - 1 else []))
            g_done = PE.n
            E("vector", "reduce_max", m1c, bank[:, 0:8], axis=AX.X, negate=True,
              wait=[(PE, g_done)], incs=[inc(DV)])
            E("scalar", "activation", tiny, bank[:, 0:8], AF.Exp, bias=m1c,
              wait=[(DV, DV.n)], incs=[inc(AC)])
            gate_ac[tau % 2] = AC.n
            for meth, args in [
                ("reduce_sum", (s1c, tiny)),
                ("reciprocal", (s2c, s1c)),
                ("tensor_scalar_mul", (tiny, tiny, s2c)),
                ("tensor_add", (tiny2, tiny, gateb_b)),
                ("reduce_max", (m1c, tiny2)),
                ("tensor_single_scalar", (tiny3, tiny2, m1c, ALU.is_equal)),
                ("scalar_tensor_tensor", (tiny3, tiny3, -1e30, tiny2,
                                          ALU.mult, ALU.add)),
                ("reduce_max", (m2c, tiny3)),
                ("tensor_single_scalar", (tiny2, tiny2, m2c, ALU.is_ge)),
                ("tensor_mul", (rw_sb[:, tau * 8:(tau + 1) * 8], tiny2, tiny)),
                ("tensor_copy", (rwx[:, 0:8], rw_sb[:, tau * 8:(tau + 1) * 8])),
            ]:
                kwx = {"axis": AX.X} if meth.startswith("reduce") else {}
                w = [(DV, DV.n)]
                if meth == "reduce_sum":
                    w.append((AC, AC.n))
                E("vector", meth, *args, wait=w, incs=[inc(DV)], **kwx)
            E("vector", "memset", rwx[:, 9:16], 0.0)
            E("vector", "memset", rwx[:, 8:9], 1.0, incs=[inc(DV)])
            rwx_done = DV.n
            gate_dv[tau % 2] = rwx_done
            ptb2 = pt[1 - tau % 2]
            E("tensor", "transpose", ptb2[0:16, 0:128], rwx, idbf_sb,
              wait=[(DV, rwx_done), (AC, pt_ac[1 - tau % 2])], incs=[inc(PE)])
            E("scalar", "copy", rwT[:, tau * 128:(tau + 1) * 128],
              ptb2[0:9, 0:128], wait=[(PE, PE.n)], incs=[inc(AC)])
            pt_ac[1 - tau % 2] = AC.n
            for ncol in range(2):
                i2 = (tau * 2 + ncol) % 2
                vbank = pb[i2]
                for ki in range(NKI):
                    w = []
                    if ki == 0:
                        w = wv_done + [(DV, v_dv[i2]), (AC, xnT_ac[tau])]
                    E("tensor", "matmul", vbank[:, 0:512],
                      xnT[:, ki * S + tau * 128: ki * S + (tau + 1) * 128],
                      wvsb[:, ki * 1024 + ncol * 512: ki * 1024 + (ncol + 1) * 512],
                      start=(ki == 0), stop=False, wait=w)
                E("tensor", "matmul", vbank[:, 0:512], onesrow_sb,
                  bvr_st[:, ncol * 512:(ncol + 1) * 512],
                  start=False, stop=True, incs=[inc(PE)])
                w = [(PE, PE.n)]
                if seq > 0 and tau == 0 and ncol == 0:
                    w.append((AC, AC.n))
                E("vector", "tensor_copy",
                  v_nat[:, tau * 1024 + ncol * 512: tau * 1024 + (ncol + 1) * 512],
                  vbank[:, 0:512], wait=w, incs=[inc(DV)])
                v_dv[i2] = DV.n
                vmark[tau] = DV.n

        proj_ac = {0: 0, 1: 0}
        dst_dv = {0: 0, 1: 0}
        z_ac = {0: 0, 1: 0}
        pv_pe = {}
        sq_ac = {}
        pair_pe = {0: 0, 1: 0}
        QKBUF = {0: (q1b, sqb, t1buf, t2buf, rstd_f, rstd_b),
                 1: (q1b_k, sqb_k, t1buf_k, t2buf_k, rstd_f_k, rstd_b_k)}
        SBANK = {(0, 0): pb[2], (0, 1): pb[2], (1, 0): pb[3], (1, 1): pb[3]}
        wqk_tok = {}
        wqsl = wqs_h.rearrange("p k (pp o) -> p k pp o", pp=NPAIR)
        wksl = wks_h.rearrange("p k (pp o) -> p k pp o", pp=NPAIR)
        for pair in range(2):
            w0 = [(PE, PE.n)] if pair == 0 else []
            if seq == 0 and pair == 0:
                w0 = w0 + wq_cast + wk_cast
            wqk_tok[pair] = wmax([
                QS.dma(wqp2[pair % 2], wqsl[:, :, pair, :], wait=w0),
                QS.dma(wkp2[pair % 2], wksl[:, :, pair, :])])
        for pair in range(NPAIR):
            wqk_done = wqk_tok[pair]
            wqp, wkp = wqp2[pair % 2], wkp2[pair % 2]
            kadd = {0: 0, 1: 0}
            for c in range(4):
                pdone = {}
                for qk in range(2):
                    wsel = wqp if qk == 0 else wkp
                    for ki in range(NKI):
                        w = []
                        if ki == 0:
                            w = wqk_done + [(AC, max(proj_ac[qk], xnT_ac[c * 4 + 3],
                                                     sq_ac.get((qk, 1), 0))),
                                            (DV, max(dst_dv[qk],
                                                     vmark[NT - 1] if (pair, c) == (0, 0) else 0))]
                        E("tensor", "matmul", pb[qk][:, 0:512],
                          wsel[:, ki * 128:(ki + 1) * 128],
                          xnT[:, ki * S + c * 512: ki * S + (c + 1) * 512],
                          start=(ki == 0), stop=(ki == NKI - 1), wait=w,
                          incs=([inc(PE)] if ki == NKI - 1 else []))
                    pdone[qk] = PE.n
                if c == 3 and pair + 2 < NPAIR:
                    p2 = pair + 2
                    wqk_tok[p2] = wmax([
                        QS.dma(wqp2[p2 % 2], wqsl[:, :, p2, :],
                               wait=[(PE, pdone[1])]),
                        QS.dma(wkp2[p2 % 2], wksl[:, :, p2, :])])
                st = {}
                for qk in range(2):
                    bsel = bqt_sb if qk == 0 else bkt_sb
                    _q1 = QKBUF[qk][0]
                    E("scalar", "activation", _q1, pb[qk][:, 0:512], AF.Identity,
                      bias=bsel[:, pair:pair + 1],
                      wait=[(PE, pdone[qk]), (DV, dst_dv[qk])])
                for qk in range(2):
                    bsel = bqt_sb if qk == 0 else bkt_sb
                    _sq = QKBUF[qk][1]
                    E("scalar", "activation", _sq, pb[qk][:, 0:512], AF.Square,
                      bias=bsel[:, pair:pair + 1], incs=[inc(AC)])
                    proj_ac[qk] = AC.n
                for qk in range(2):
                    E("tensor", "matmul", pb[qk][0:2, 0:512], ones2_sb,
                      QKBUF[qk][1], start=True, stop=True,
                      wait=[(AC, proj_ac[qk])], incs=[inc(PE)])
                    st[qk] = PE.n
                for qk in range(2):
                    E("scalar", "activation", QKBUF[qk][4], pb[qk][0:2, 0:512],
                      AF.Ln, scale=1.0 / DH, bias=RMS_EPS, wait=[(PE, st[qk])])
                for qk in range(2):
                    E("scalar", "activation", QKBUF[qk][4], QKBUF[qk][4], AF.Exp,
                      scale=-0.5, bias=(-math.log(DH) if qk == 0 else 0.0),
                      incs=[inc(AC)])
                    st[qk] = AC.n
                for qk in range(2):
                    E("vector", "tensor_copy", QKBUF[qk][5], QKBUF[qk][4],
                      wait=[(AC, st[qk])], incs=[inc(DV)])
                    st[qk] = DV.n
                for qk in range(2):
                    E("tensor", "matmul", pb[qk][:, 0:512], sel2_sb, QKBUF[qk][5],
                      start=True, stop=True, wait=[(DV, st[qk])], incs=[inc(PE)])
                    st[qk] = PE.n
                for qk in range(2):
                    E("vector", "tensor_tensor", QKBUF[qk][1], QKBUF[qk][0],
                      pb[qk][:, 0:512], ALU.mult, wait=[(PE, st[qk])],
                      incs=[inc(DV)])
                    st[qk] = DV.n
                for qk in range(2):
                    E("tensor", "matmul", pb[qk][:, 0:512], prot_sb, QKBUF[qk][1],
                      start=True, stop=True, wait=[(DV, st[qk])], incs=[inc(PE)])
                    st[qk] = PE.n
                    if qk == 1 and c == 3:
                        pair_pe[pair % 2] = PE.n
                for qk in range(2):
                    E("vector", "tensor_tensor", QKBUF[qk][3], pb[qk][:, 0:512],
                      sw[:, c * 512:(c + 1) * 512], ALU.mult, wait=[(PE, st[qk])])
                for qk in range(2):
                    E("vector", "tensor_tensor", QKBUF[qk][2], QKBUF[qk][1],
                      cw[:, c * 512:(c + 1) * 512], ALU.mult)
                for qk in range(2):
                    dst = qTp if qk == 0 else kTp
                    E("vector", "tensor_add", dst[:, c * 512:(c + 1) * 512],
                      QKBUF[qk][2], QKBUF[qk][3], incs=[inc(DV)])
                    dst_dv[qk] = DV.n
                    kadd[qk] = DV.n
                qc = c
                nkt = qc * 4 + 4
                for kt in range(nkt):
                    patt = {}
                    for hh in range(2):
                        sbank = SBANK[(hh, kt % 2)]
                        pbf = pbuf[2 * hh + (kt % 2)]
                        w = [(AC, sq_ac.get((hh, kt % 2), 0))]
                        if kt <= 1:
                            w.append((DV, max(kadd[0], kadd[1])))
                        E("tensor", "matmul", sbank[:, 0:512],
                          kTp[64 * hh:64 * hh + 64, kt * 128:(kt + 1) * 128],
                          qTp[64 * hh:64 * hh + 64, qc * 512:(qc + 1) * 512],
                          start=True, stop=True, tile_position=(64 * hh, 0),
                          skip_group_check=True, wait=w, incs=[inc(PE)])
                        s_done = PE.n
                        wsq = [(PE, s_done), (PE, pv_pe.get((hh, kt % 2), 0))]
                        E("scalar", "activation", pbf, sbank[:, 0:512],
                          AF.Square, wait=wsq, incs=[inc(AC)])
                        sq_ac[(hh, kt % 2)] = AC.n
                        if kt >= qc * 4:
                            off = (kt - qc * 4) * 128
                            if off > 0:
                                E("vector", "memset", pbf[:, 0:off], 0.0,
                                  wait=[(AC, AC.n)])
                            E("vector", "tensor_tensor", pbf[:, off:off + 128],
                              pbf[:, off:off + 128], tri_sb, ALU.mult,
                              wait=([(AC, AC.n)] if off == 0 else []),
                              incs=[inc(DV)])
                            patt[hh] = (DV, DV.n)
                        else:
                            patt[hh] = (AC, AC.n)
                    for hh in range(2):
                        zbank = pb[4 + hh]
                        pbf = pbuf[2 * hh + (kt % 2)]
                        w = [patt[hh], (DV, vmark[kt])]
                        if kt == 0:
                            w.append((AC, z_ac[hh]))
                        E("tensor", "matmul", zbank[0:64, 0:512],
                          v_nat[:, kt * 1024 + (2 * pair + hh) * 64:
                                kt * 1024 + (2 * pair + hh) * 64 + 64],
                          pbf, start=(kt == 0), stop=(kt == nkt - 1),
                          skip_group_check=True, wait=w, incs=[inc(PE)])
                        pv_pe[(hh, kt % 2)] = PE.n
                for hh in range(2):
                    E("scalar", "copy",
                      zT[64 * hh:64 * hh + 64,
                         pair * S + qc * 512: pair * S + (qc + 1) * 512],
                      pb[4 + hh][0:64, 0:512],
                      wait=[(PE, pv_pe[(hh, (nkt - 1) % 2)])], incs=[inc(AC)])
                    z_ac[hh] = AC.n
        barrier()

        for half in range(2):
            t0 = half * HALF
            yinit_dv = {0: 0, 1: 0}
            for tl in range(NTL):
                tau = half * NTL + tl
                for ncol in range(2):
                    i2 = (tl * 2 + ncol) % 2
                    bank = pb[4 + i2]
                    E("tensor", "matmul", bank[:, 0:512],
                      rwT[:, tau * 128:(tau + 1) * 128],
                      bp_sb[:, ncol * 512:(ncol + 1) * 512],
                      start=True, stop=True,
                      wait=([(DV, yinit_dv[i2])] if yinit_dv[i2] else []),
                      incs=[inc(PE)])
                    E("vector", "tensor_copy",
                      y_acc[:, tl * 1024 + ncol * 512: tl * 1024 + (ncol + 1) * 512],
                      bank[:, 0:512], wait=[(PE, PE.n)], incs=[inc(DV)])
                    yinit_dv[i2] = DV.n
            oe_dv[0] = oe_dv[1] = DV.n
            for e in range(NE):
                silu_first = True
                for mo in range(NMO):
                    sb_i = mo % 2
                    w = [(PE, slab_g[sb_i])]
                    if seq == 0 and half == 0 and mo == 0:
                        w = w + wcast_tok[e]
                    t1_ = QW.dma(w1m[sb_i], w1s_h[e, :, mo * 1024:(mo + 1) * 1024],
                                 wait=w)
                    t3_ = QS.dma(w3m[sb_i], w3s_h[e, :, mo * 1024:(mo + 1) * 1024],
                                 wait=w)
                    slab_dma = wmax([t1_, t3_])
                    for ncol in range(2):
                        i2 = (mo * 2 + ncol) % 2
                        g1bank, g3bank = pb[0 + i2], pb[2 + i2]
                        for ki in range(NKI):
                            w = []
                            if ki == 0:
                                w = slab_dma + [(DV, h_dv[i2])]
                            E("tensor", "matmul", g1bank[:, 0:512],
                              w1m[sb_i][:, ki * 128:(ki + 1) * 128],
                              xnT[:, ki * S + t0 + ncol * 512:
                                  ki * S + t0 + (ncol + 1) * 512],
                              start=(ki == 0), stop=(ki == NKI - 1), wait=w)
                        for ki in range(NKI):
                            E("tensor", "matmul", g3bank[:, 0:512],
                              w3m[sb_i][:, ki * 128:(ki + 1) * 128],
                              xnT[:, ki * S + t0 + ncol * 512:
                                  ki * S + t0 + (ncol + 1) * 512],
                              start=(ki == 0), stop=(ki == NKI - 1),
                              incs=([inc(PE)] if ki == NKI - 1 else []))
                        gg_done = PE.n
                        if ncol == 1:
                            slab_g[sb_i] = gg_done
                        E("vector", "tensor_scalar_add", t3b, g3bank[:, 0:512],
                          b3t_sb[:, e * NMO + mo: e * NMO + mo + 1],
                          wait=[(PE, gg_done), (AC, h_ac[i2])], incs=[inc(DV)])
                        E("vector", "scalar_tensor_tensor", prodb[i2],
                          g1bank[:, 0:512],
                          b1t_sb[:, e * NMO + mo: e * NMO + mo + 1], t3b,
                          ALU.add, ALU.mult, wait=[(DV, DV.n)], incs=[inc(DV)])
                        h_dv[i2] = DV.n
                        wsilu = [(DV, DV.n)]
                        if silu_first:
                            wsilu.append((PE, oe_last_pe))
                            wsilu += QS.drain_tokens()
                            silu_first = False
                        E("scalar", "activation",
                          hT[:, mo * 1024 + ncol * 512: mo * 1024 + (ncol + 1) * 512],
                          prodb[i2], AF.Silu, wait=wsilu, incs=[inc(AC)])
                        h_ac[i2] = AC.n
                h_done_ac = AC.n
                w2_toks = []
                for hi in range(2):
                    off = 0 if hi == 0 else 6 * 1024
                    nel = 6 * 1024 if hi == 0 else 5 * 1024
                    w = []
                    if hi == 0:
                        w = [(PE, oe_last_pe)]
                        if seq == 0 and half == 0:
                            w = w + wcast_tok[e]
                    w2_toks.append(QS.dma(w2sb[:, off:off + nel],
                                          w2s_h[e, :, off:off + nel], wait=w))
                w2_dma = wmax(w2_toks)
                for tl in range(NTL):
                    tau = half * NTL + tl
                    for ncol in range(2):
                        i2 = (tl * 2 + ncol) % 2
                        bank = pb[4 + i2]
                        for kh in range(NMO):
                            w = []
                            if kh == 0:
                                w = w2_dma + [(AC, h_done_ac), (DV, oe_dv[i2])]
                            E("tensor", "matmul", bank[:, 0:512],
                              hT[:, kh * 1024 + tl * 128: kh * 1024 + (tl + 1) * 128],
                              w2sb[:, kh * 1024 + ncol * 512:
                                   kh * 1024 + (ncol + 1) * 512],
                              start=(kh == 0), stop=(kh == NMO - 1), wait=w,
                              incs=([inc(PE)] if kh == NMO - 1 else []))
                        oe_last_pe = PE.n
                        ysl = y_acc[:, tl * 1024 + ncol * 512:
                                    tl * 1024 + (ncol + 1) * 512]
                        if e < 8:
                            E("vector", "scalar_tensor_tensor", ysl, bank[:, 0:512],
                              rw_sb[:, tau * 8 + e: tau * 8 + e + 1], ysl,
                              ALU.mult, ALU.add, wait=[(PE, oe_last_pe)],
                              incs=[inc(DV)])
                        else:
                            E("vector", "tensor_add", ysl, ysl, bank[:, 0:512],
                              wait=[(PE, oe_last_pe)], incs=[inc(DV)])
                        oe_dv[i2] = DV.n
            barrier()
            wo_toks = [QW.dma(wosb, wos_h,
                              wait=[(PE, oe_last_pe)]
                              + (wo_cast if seq == 0 and half == 0 else []))]
            wo_dma = wmax(wo_toks)
            fin_dv = {0: 0, 1: 0}
            fin_st = {0: [], 1: []}
            fin_nc = {0: 0, 1: 0}
            last_fin_dv = 0
            for tl in range(NTL):
                tau = half * NTL + tl
                b = tl % 2
                xload = QX.dma(xt[b], x_h[seq, tau * 128:(tau + 1) * 128, :],
                               wait=([(DV, xt_guard[b])] if xt_guard[b] else []))
                E("scalar", "activation", gbuf, y_acc[:, tl * 1024:(tl + 1) * 1024],
                  AF.Silu, wait=[(DV, last_fin_dv)], incs=[inc(AC)])
                g_done = AC.n
                swo_done = {}
                for ncol in range(2):
                    i2 = (tl * 2 + ncol) % 2
                    bank = pb[4 + i2]
                    for pp in range(NKI):
                        w = []
                        if pp == 0:
                            w = wo_dma + [(DV, fin_nc[i2] or oe_dv[i2])]
                        E("tensor", "matmul", bank[:, 0:512],
                          zT[:, pp * S + tau * 128: pp * S + (tau + 1) * 128],
                          wosb[:, pp * 1024 + ncol * 512: pp * 1024 + (ncol + 1) * 512],
                          start=(pp == 0), stop=(pp == NKI - 1), wait=w,
                          incs=([inc(PE)] if pp == NKI - 1 else []))
                    swo_done[ncol] = PE.n
                for ncol in range(2):
                    E("vector", "tensor_scalar",
                      sbuf_s[:, ncol * 512:(ncol + 1) * 512],
                      xt[b][:, ncol * 512:(ncol + 1) * 512],
                      stat_m[:, tau:tau + 1], stat_r[:, tau:tau + 1],
                      ALU.subtract, ALU.mult, wait=([xload] if ncol == 0 else []))
                for ncol in range(2):
                    E("vector", "tensor_add", sbuf_s[:, ncol * 512:(ncol + 1) * 512],
                      sbuf_s[:, ncol * 512:(ncol + 1) * 512],
                      pb[4 + (tl * 2 + ncol) % 2][:, 0:512],
                      wait=[(PE, swo_done[ncol])],
                      incs=([inc(DV)] if ncol == 1 else []))
                fin_nc[tl * 2 % 2] = DV.n
                fin_nc[(tl * 2 + 1) % 2] = DV.n
                for ncol in range(2):
                    E("vector", "tensor_mul", outb[b][:, ncol * 512:(ncol + 1) * 512],
                      sbuf_s[:, ncol * 512:(ncol + 1) * 512],
                      gbuf[:, ncol * 512:(ncol + 1) * 512],
                      wait=([(AC, g_done)] + fin_st[b] if ncol == 0 else []))
                for ncol in range(2):
                    E("vector", "tensor_add", outb[b][:, ncol * 512:(ncol + 1) * 512],
                      outb[b][:, ncol * 512:(ncol + 1) * 512],
                      xt[b][:, ncol * 512:(ncol + 1) * 512],
                      incs=([inc(DV)] if ncol == 1 else []))
                fin_dv[b] = DV.n
                xt_guard[b] = DV.n
                last_fin_dv = DV.n
                fin_st[b] = [QO.dma(out_h[seq, tau * 128:(tau + 1) * 128, :],
                                    outb[b], wait=[(DV, fin_dv[b])])]
            barrier()

    if DEBUG:
        barrier()
        for nm, buf in [("d_cw", cw), ("d_sw", sw), ("d_statm", stat_m),
                        ("d_statr", stat_r), ("d_xnT", xnT), ("d_zT", zT),
                        ("d_rw", rw_sb), ("d_rwT", rwT), ("d_y", y_acc),
                        ("d_gate", gate_sb)]:
            QO.dma(dbg[nm], buf)
    E("sync", "nop", wait=QO.drain_tokens() + QS.drain_tokens())

    with nc.Block() as block:
        block.sync(lambda e: p.replay(e, "sync"))
        block.gpsimd(lambda e: p.replay(e, "gpsimd"))
        block.tensor(lambda e: p.replay(e, "tensor"))
        block.vector(lambda e: p.replay(e, "vector"))
        block.scalar(lambda e: p.replay(e, "scalar"))
    return nc


def _layout_inputs(i, **inp):
    f = lambda a: np.asarray(a, np.float32)
    nw = f(inp["L0_attn_norm_w"])
    nws = np.ascontiguousarray(nw.reshape(32, 2)[:, ::-1].reshape(64))
    c = np.ascontiguousarray
    wslab = lambda w: c(f(w).T.reshape(NKI, 128, D).transpose(1, 0, 2))
    return {
        "x": c(f(inp["L0_attn_x"])[2 * i:2 * i + 2]),
        "wqt": wslab(inp["L0_attn_Wq"]),
        "wkt": wslab(inp["L0_attn_Wk"]),
        "wvt": wslab(inp["L0_attn_Wv"]),
        "wot": wslab(inp["L0_attn_Wo"]),
        "bqt": c(f(inp["L0_attn_bq"]).reshape(8, 128).T),
        "bkt": c(f(inp["L0_attn_bk"]).reshape(8, 128).T),
        "bvr": c(f(inp["L0_attn_bv"]).reshape(1, D)),
        "nw": c(np.tile(nw, 2).reshape(128, 1)),
        "nws": c(np.tile(nws, 2).reshape(128, 1)),
        "gwt": c(f(inp["L0_ffn_gate_w"]).T.reshape(NKI, 128, 8).transpose(1, 0, 2).reshape(128, NKI * 8)),
        "gb": c(f(inp["L0_ffn_gate_b"]).reshape(1, 8)),
        "b1t": c(np.concatenate(
            [f(inp["L0_ffn_b1"]).reshape(8, NMO, 128).transpose(0, 2, 1),
             f(inp["L0_ffn_sb1"]).reshape(NMO, 128).T[None]], 0)
            .transpose(1, 0, 2).reshape(128, NE * NMO)),
        "b3t": c(np.concatenate(
            [f(inp["L0_ffn_b3"]).reshape(8, NMO, 128).transpose(0, 2, 1),
             f(inp["L0_ffn_sb3"]).reshape(NMO, 128).T[None]], 0)
            .transpose(1, 0, 2).reshape(128, NE * NMO)),
        "bp": c(np.concatenate(
            [f(inp["L0_ffn_b2"]), f(inp["L0_ffn_sb2"]).reshape(1, D)], 0)),
        "w1": c(f(inp["L0_ffn_w1"]).reshape(8, NKI, 128, H).transpose(0, 2, 1, 3)),
        "w3": c(f(inp["L0_ffn_w3"]).reshape(8, NKI, 128, H).transpose(0, 2, 1, 3)),
        "w2": c(f(inp["L0_ffn_w2"]).reshape(8, NMO, 128, D).transpose(0, 2, 1, 3)),
        "sw1": c(f(inp["L0_ffn_sw1"]).reshape(NKI, 128, H).transpose(1, 0, 2)),
        "sw3": c(f(inp["L0_ffn_sw3"]).reshape(NKI, 128, H).transpose(1, 0, 2)),
        "sw2": c(f(inp["L0_ffn_sw2"]).reshape(NMO, 128, D).transpose(1, 0, 2)),
    }


_NC_CACHE = None


def kernel(**inputs):
    global _NC_CACHE
    if _NC_CACHE is None:
        _NC_CACHE = build_nc()
    in_maps = [_layout_inputs(i, **inputs) for i in range(NCORE)]
    res = run_bass_kernel_spmd(_NC_CACHE, in_maps, core_ids=list(range(NCORE)))
    if DEBUG:
        import pickle
        with open("/root/problem/_dbg_dumps.pkl", "wb") as fh:
            pickle.dump([{k: np.asarray(v) for k, v in r.items()} for r in res.results], fh)
    return np.concatenate([res.results[i]["out"] for i in range(NCORE)], axis=0)


if __name__ == "__main__":
    build_nc()
    print("built ok")


# version 22 (final)
# speedup vs baseline: 1.1935x; 1.1935x over previous
import math
import numpy as np
import ml_dtypes
import concourse.bass as bass
import concourse.mybir as mybir
from concourse.bass_utils import run_bass_kernel_spmd

F32 = mybir.dt.float32
BF16 = mybir.dt.bfloat16
U8 = mybir.dt.uint8
AF = mybir.ActivationFunctionType
ALU = mybir.AluOpType
AX = mybir.AxisListType

B, S, D = 16, 2048, 1024
NCORE = 8
BLOC = B // NCORE
NH, DH, NPAIR = 16, 64, 8
NT = S // 128
H = 1408
NMO = H // 128
NE = 9
NKI = D // 128
HALF = 1024
NTL = HALF // 128
LN_EPS = 1e-5
RMS_EPS = float(np.finfo(np.float32).eps)

DEBUG = False
ENGS = ("sync", "gpsimd", "tensor", "vector", "scalar")


class Sem:
    __slots__ = ("h", "n")

    def __init__(self, h):
        self.h = h
        self.n = 0


class Prog:
    def __init__(self):
        self.ops = {e: [] for e in ENGS}

    def emit(self, eng, meth, *args, wait=None, incs=None, **kw):
        self.ops[eng].append((meth, args, kw, list(wait or []), list(incs or [])))

    def replay(self, eng_obj, name):
        for meth, args, kw, waits, incs in self.ops[name]:
            for s, v in waits:
                if v > 0:
                    eng_obj.wait_ge(s.h, v)
            r = getattr(eng_obj, meth)(*args, **kw)
            for s, k in incs:
                r.then_inc(s.h, k)


def _rope_tables():
    inv = (1.0 / (10000.0 ** (np.arange(0, DH, 2, dtype=np.float32) / DH))).astype(np.float32)
    t = np.arange(S, dtype=np.float32)
    ang = (inv[:, None] * t[None, :]).astype(np.float32)
    cos = np.cos(ang).astype(np.float32)
    sin = np.sin(ang).astype(np.float32)
    cosT = np.concatenate([cos, cos], 0)
    sinT = np.concatenate([sin, sin], 0)
    return np.tile(cosT, (2, 1)), np.tile(sinT, (2, 1))


COS128, SIN128 = _rope_tables()
ID_BF = np.eye(128, dtype=ml_dtypes.bfloat16)
PROT = np.zeros((128, 128), dtype=ml_dtypes.bfloat16)
for _i in range(64):
    PROT[2 * _i + 1, 2 * _i] = -1.0
    PROT[2 * _i, 2 * _i + 1] = 1.0
TRI = np.triu(np.ones((128, 128), dtype=np.float32)).astype(ml_dtypes.bfloat16)
ONES2 = np.zeros((128, 2), dtype=ml_dtypes.bfloat16)
ONES2[0:64, 0] = 1.0
ONES2[64:128, 1] = 1.0
SEL2 = np.zeros((2, 128), dtype=ml_dtypes.bfloat16)
SEL2[0, 0:64] = 1.0
SEL2[1, 64:128] = 1.0
ONESROW = np.ones((1, 128), dtype=ml_dtypes.bfloat16)
CONSTVALS = [LN_EPS, RMS_EPS, -math.log(DH), 1.0 / DH, -0.5, 1.0 / 1024.0, -1e30]
CONSTNP = np.tile(np.array([CONSTVALS], dtype=np.float32), (128, 1))


def build_nc():
    nc = bass.Bass("TRN2", target_bir_lowering=False, detect_race_conditions=False)

    x_h = nc.dram_tensor("x", [BLOC, S, D], F32, kind="ExternalInput").ap()
    wqt_h = nc.dram_tensor("wqt", [128, NKI, D], F32, kind="ExternalInput").ap()
    wkt_h = nc.dram_tensor("wkt", [128, NKI, D], F32, kind="ExternalInput").ap()
    wvt_h = nc.dram_tensor("wvt", [128, NKI, D], F32, kind="ExternalInput").ap()
    wot_h = nc.dram_tensor("wot", [128, NKI, D], F32, kind="ExternalInput").ap()
    bqt_h = nc.dram_tensor("bqt", [128, NPAIR], F32, kind="ExternalInput").ap()
    bkt_h = nc.dram_tensor("bkt", [128, NPAIR], F32, kind="ExternalInput").ap()
    bvr_h = nc.dram_tensor("bvr", [1, D], F32, kind="ExternalInput").ap()
    nw_h = nc.dram_tensor("nw", [128, 1], F32, kind="ExternalInput").ap()
    nws_h = nc.dram_tensor("nws", [128, 1], F32, kind="ExternalInput").ap()
    gwt_h = nc.dram_tensor("gwt", [128, NKI * 8], F32, kind="ExternalInput").ap()
    gb_h = nc.dram_tensor("gb", [1, 8], F32, kind="ExternalInput").ap()
    b1t_h = nc.dram_tensor("b1t", [128, NE * NMO], F32, kind="ExternalInput").ap()
    b3t_h = nc.dram_tensor("b3t", [128, NE * NMO], F32, kind="ExternalInput").ap()
    bp_h = nc.dram_tensor("bp", [NE, D], F32, kind="ExternalInput").ap()
    w1_h = nc.dram_tensor("w1", [8, 128, NKI, H], F32, kind="ExternalInput").ap()
    w3_h = nc.dram_tensor("w3", [8, 128, NKI, H], F32, kind="ExternalInput").ap()
    w2_h = nc.dram_tensor("w2", [8, 128, NMO, D], F32, kind="ExternalInput").ap()
    sw1_h = nc.dram_tensor("sw1", [128, NKI, H], F32, kind="ExternalInput").ap()
    sw3_h = nc.dram_tensor("sw3", [128, NKI, H], F32, kind="ExternalInput").ap()
    sw2_h = nc.dram_tensor("sw2", [128, NMO, D], F32, kind="ExternalInput").ap()
    out_h = nc.dram_tensor("out", [BLOC, S, D], F32, kind="ExternalOutput").ap()
    dbg = {}
    if DEBUG:
        for nm, shp, dt in [("d_cw", [128, S], BF16), ("d_sw", [128, S], F32),
                            ("d_statm", [128, NT], F32), ("d_statr", [128, NT], F32),
                            ("d_xnT", [128, NKI * S], BF16), ("d_zT", [128, NPAIR * S], BF16),
                            ("d_rw", [128, NT * 8], F32), ("d_rwT", [9, NT * 128], BF16),
                            ("d_u1", [128, 32768], U8),
                            ("d_y", [128, 8192], F32), ("d_gate", [128, NKI * 8], BF16),
                            ("d_qT", [128, S], BF16), ("d_kT", [128, S], BF16),
                            ("d_v", [128, 16384], BF16)]:
            dbg[nm] = nc.dram_tensor(nm, shp, dt, kind="ExternalOutput").ap()

    cos_c = nc.inline_tensor(COS128, "cos128").ap()
    sin_c = nc.inline_tensor(SIN128, "sin128").ap()
    idbf_c = nc.inline_tensor(ID_BF, "idbf").ap()
    prot_c = nc.inline_tensor(PROT, "prot").ap()
    tri_c = nc.inline_tensor(TRI, "tri").ap()
    ones2_c = nc.inline_tensor(ONES2, "ones2").ap()
    sel2_c = nc.inline_tensor(SEL2, "sel2").ap()
    onesrow_c = nc.inline_tensor(ONESROW, "onesrow").ap()
    cvals_c = nc.inline_tensor(CONSTNP, "cvals").ap()

    w1s_h = nc.dram_tensor("w1s", [NE, 128, NMO * 1024], BF16, kind="Internal").ap()
    w3s_h = nc.dram_tensor("w3s", [NE, 128, NMO * 1024], BF16, kind="Internal").ap()
    w2s_h = nc.dram_tensor("w2s", [NE, 128, NMO * 1024], BF16, kind="Internal").ap()
    wos_h = nc.dram_tensor("wos", [128, NKI * 1024], BF16, kind="Internal").ap()
    wqs_h = nc.dram_tensor("wqs", [128, NKI, 1024], BF16, kind="Internal").ap()
    wks_h = nc.dram_tensor("wks", [128, NKI, 1024], BF16, kind="Internal").ap()
    wvs_h = nc.dram_tensor("wvs", [128, NKI, 1024], BF16, kind="Internal").ap()

    def A(nm, shp, dt=F32):
        return nc.alloc_sbuf_tensor(nm, shp, dt).ap()

    xnT = A("xnT", [128, NKI * S], BF16)
    zT = A("zT", [128, NPAIR * S], BF16)
    u1 = nc.alloc_sbuf_tensor("u1", [128, 32768], U8)
    v_nat = u1.ap().bitcast(BF16)
    y_acc = u1.ap().bitcast(F32)
    u2 = nc.alloc_sbuf_tensor("u2", [128, 59392], U8)
    u2b = u2.ap().bitcast(BF16)
    u2f = u2.ap().bitcast(F32)

    def cb(off, n):
        assert off % 2 == 0
        return u2b[:, off // 2: off // 2 + n]

    def cf(off, n):
        assert off % 4 == 0
        return u2f[:, off // 4: off // 4 + n]

    hT = cb(0, NMO * 1024)
    w2sb = cb(22528, NMO * 1024)
    w1m = [cb(45056 + i * 2048, 1024) for i in range(2)]
    w3m = [cb(49152 + i * 2048, 1024) for i in range(2)]
    t3b = cf(53248, 512)
    prodb = [cf(55296 + i * 2048, 512) for i in range(2)]
    qTp = cb(0, S)
    kTp = cb(4096, S)
    wqp2 = [cb(8192 + i * 2048, NKI * 128) for i in range(2)]
    wvsb = cb(12288, NKI * 1024)
    wkp2 = [cb(12288 + i * 2048, NKI * 128) for i in range(2)]
    pbuf = [cb(28672 + i * 1024, 512) for i in range(4)]
    q1b = cf(32768, 512)
    t1buf = cb(34816, 512)
    t2buf = cb(35840, 512)
    sqb = cb(36864, 512)
    rstd_f = cf(37888, 512)[0:2, :]
    rstd_b = cb(39936, 512)[0:2, :]
    cosst = cf(0, 2048)
    bvr_sb = cb(40960, 1024)[0:1, :]
    q1b_k = cf(16384, 512)
    sqb_k = cb(18432, 512)
    t1buf_k = cb(19456, 512)
    t2buf_k = cb(20480, 512)
    rstd_f_k = cf(21504, 512)[0:2, :]
    rstd_b_k = cb(23552, 512)[0:2, :]
    wosb = cb(0, NKI * 1024)

    wstage = [A(f"wstage{i}", [128, 1024], BF16) for i in range(2)]
    bvr_st = A("bvr_st", [1, D], BF16)
    cw = A("cw", [128, S], BF16)
    sw = A("sw", [128, S], F32)
    xt = [A(f"xt{i}", [128, D], F32) for i in range(2)]
    outb = [A(f"outb{i}", [128, D], F32) for i in range(2)]
    gbuf = A("gbuf", [128, D], F32)
    sbuf_s = A("sbufs", [128, D], F32)
    stat_m = A("stat_m", [128, NT], F32)
    stat_r = A("stat_r", [128, NT], F32)
    rw_sb = A("rw_sb", [128, NT * 8], F32)
    rwT = A("rwT", [9, NT * 128], BF16)
    rwx = A("rwx", [128, 16], BF16)
    gate_sb = A("gate_sb", [128, NKI * 8], BF16)
    gateb_b = A("gateb_b", [128, 8], F32)
    gb_sb = A("gb_sb", [1, 8], BF16)
    bqt_sb = A("bqt_sb", [128, NPAIR], F32)
    bkt_sb = A("bkt_sb", [128, NPAIR], F32)
    nw_sb = A("nw_sb", [128, 1], F32)
    nws_sb = A("nws_sb", [128, 1], F32)
    b1t_sb = A("b1t_sb", [128, NE * NMO], F32)
    b3t_sb = A("b3t_sb", [128, NE * NMO], F32)
    bp_sb = A("bp_sb", [NE, D], BF16)
    idbf_sb = A("idbf_sb", [128, 128], BF16)
    prot_sb = A("prot_sb", [128, 128], BF16)
    tri_sb = A("tri_sb", [128, 128], BF16)
    ones2_sb = A("ones2_sb", [128, 2], BF16)
    sel2_sb = A("sel2_sb", [2, 128], BF16)
    onesrow_sb = A("onesrow_sb", [1, 128], BF16)
    tiny = A("tiny", [128, 8], F32)
    tiny2 = A("tiny2", [128, 8], F32)
    tiny3 = A("tiny3", [128, 8], F32)
    cst_sb = A("cst_sb", [128, len(CONSTVALS)], F32)
    for _ci, _cv in enumerate(CONSTVALS):
        nc.const_aps.register(F32, float(_cv), cst_sb[:, _ci:_ci + 1])
    m1c = A("m1c", [128, 1], F32)
    m2c = A("m2c", [128, 1], F32)
    s1c = A("s1c", [128, 1], F32)
    s2c = A("s2c", [128, 1], F32)

    pb = [nc.alloc_psum_tensor(f"pb{i}", [128, 512], F32).ap() for i in range(6)]
    pt = [nc.alloc_psum_tensor(f"pt{i}", [128, 1024], BF16).ap() for i in range(2)]

    PE = Sem(nc.alloc_semaphore("PE"))
    DV = Sem(nc.alloc_semaphore("DV"))
    AC = Sem(nc.alloc_semaphore("AC"))
    BS = {e: Sem(nc.alloc_semaphore(f"BS_{e}")) for e in ENGS}

    def inc(s, k=1):
        s.n += k
        return (s, k)

    p = Prog()
    E = p.emit

    class Dq:

        def __init__(self, name, eng, nslots):
            self.eng = eng
            self.slots = [Sem(nc.alloc_semaphore(f"{name}{i}")) for i in range(nslots)]
            self.i = 0

        def dma(self, out, in_, wait=None, **kw):
            s = self.slots[self.i % len(self.slots)]
            self.i += 1
            w = list(wait or [])
            if s.n:
                w.append((s, s.n))
            s.n += 16
            E(self.eng, "dma_start", out=out, in_=in_, wait=w,
              incs=[(s, 16)], **kw)
            return (s, s.n)

        def drain_tokens(self):
            return [(s, s.n) for s in self.slots if s.n]

    QX = Dq("QX", "sync", 4)
    QS = Dq("QS", "sync", 3)
    QW = Dq("QW", "gpsimd", 6)
    QO = Dq("QO", "sync", 3)

    def wmax(tokens):
        best = {}
        for s, v in tokens:
            if v > best.get(id(s), (None, -1))[1]:
                best[id(s)] = (s, v)
        return list(best.values())

    BENGS = ("sync", "tensor", "vector", "scalar")

    def barrier():
        for eng in BENGS:
            E(eng, "nop", incs=[inc(BS[eng], 1)])
        vals = {eng: BS[eng].n for eng in BENGS}
        for eng in BENGS:
            E(eng, "nop", wait=[(BS[o], vals[o]) for o in BENGS if o != eng])

    setup_toks = []
    for dst, srcap in [(bqt_sb, bqt_h), (bkt_sb, bkt_h), (nw_sb, nw_h),
                       (nws_sb, nws_h), (b1t_sb, b1t_h), (b3t_sb, b3t_h),
                       (idbf_sb, idbf_c), (prot_sb, prot_c), (tri_sb, tri_c),
                       (ones2_sb, ones2_c), (sel2_sb, sel2_c),
                       (onesrow_sb, onesrow_c), (cst_sb, cvals_c)]:
        setup_toks.append(QX.dma(dst, srcap))
    setup_toks.append(QW.dma(bp_sb, bp_h))
    setup_toks.append(QW.dma(gate_sb, gwt_h))
    setup_toks.append(QW.dma(gb_sb, gb_h))
    setup_toks.append(QW.dma(bvr_st, bvr_h))
    ctok = QX.dma(cosst, cos_c)
    E("vector", "tensor_scalar_mul", cw, cosst, nw_sb,
      wait=wmax(setup_toks + [ctok]), incs=[inc(DV)])
    ctok = QX.dma(cosst, sin_c, wait=[(DV, DV.n)])
    E("vector", "tensor_scalar_mul", sw, cosst, nws_sb, wait=[ctok],
      incs=[inc(DV)])
    _wsi = [0]

    def _cast_slab(dst, srcap):
        i = _wsi[0] % 2
        _wsi[0] += 1
        t = QW.dma(wstage[i], srcap)
        return QW.dma(dst, wstage[i], wait=[t])

    wv_cast = wmax([_cast_slab(wvs_h[:, ki, :], wvt_h[:, ki, :])
                    for ki in range(NKI)])
    wq_cast = wmax([_cast_slab(wqs_h[:, ki, :], wqt_h[:, ki, :])
                    for ki in range(NKI)])
    wk_cast = wmax([_cast_slab(wks_h[:, ki, :], wkt_h[:, ki, :])
                    for ki in range(NKI)])
    wo_cast = wmax([_cast_slab(wos_h[:, ki * 1024:(ki + 1) * 1024], wot_h[:, ki, :])
                    for ki in range(NKI)])
    wcast_tok = {}
    for e in range(NE):
        toks = []
        for mo in range(NMO):
            toks.append(_cast_slab(
                w1s_h[e, :, mo * 1024:(mo + 1) * 1024],
                (w1_h[e, :, :, mo * 128:(mo + 1) * 128] if e < 8
                 else sw1_h[:, :, mo * 128:(mo + 1) * 128])))
            toks.append(_cast_slab(
                w3s_h[e, :, mo * 1024:(mo + 1) * 1024],
                (w3_h[e, :, :, mo * 128:(mo + 1) * 128] if e < 8
                 else sw3_h[:, :, mo * 128:(mo + 1) * 128])))
        for kh in range(NMO):
            toks.append(_cast_slab(
                w2s_h[e, :, kh * 1024:(kh + 1) * 1024],
                w2_h[e, :, kh, :] if e < 8 else sw2_h[:, kh, :]))
        wcast_tok[e] = wmax(toks)
    E("tensor", "matmul", pb[0][:, 0:8], onesrow_sb, gb_sb, start=True, stop=True,
      wait=wmax(setup_toks), incs=[inc(PE)])
    E("vector", "tensor_copy", gateb_b, pb[0][:, 0:8], wait=[(PE, PE.n)],
      incs=[inc(DV)])
    barrier()

    slab_g = {0: 0, 1: 0}
    h_dv = {0: 0, 1: 0}
    h_ac = {0: 0, 1: 0}
    oe_dv = {0: 0, 1: 0}
    oe_last_pe = 0

    xt_guard = {0: 0, 1: 0}
    for seq in range(BLOC):
        xa_dv = {}
        pt_ac = {0: 0, 1: 0}
        tp_pe_prev = 0
        xnbf = sbuf_s.bitcast(BF16)[:, 0:1024]
        xnT_ac = {}
        vmark = {}
        v_dv = {0: 0, 1: 0}
        gate_ac = {0: 0, 1: 0}
        gate_dv = {0: 0, 1: 0}
        wv_toks = [QS.dma(wvsb[:, ki * 1024:(ki + 1) * 1024], wvs_h[:, ki, :],
                          wait=(wv_cast + [(PE, PE.n)] if seq == 0 and ki == 0
                                else ([(PE, PE.n)] if ki == 0 else [])))
                   for ki in range(NKI)]
        wv_done = wmax(wv_toks)
        for tau in range(NT):
            b = tau % 2
            xload = QX.dma(xt[b], x_h[seq, tau * 128:(tau + 1) * 128, :],
                           wait=([(DV, xt_guard[b])] if xt_guard[b] else []))
            E("scalar", "activation", gbuf, xt[b], AF.Square,
              wait=[xload], incs=[inc(AC)])
            sq_done = AC.n
            E("vector", "reduce_sum", s1c, xt[b], axis=AX.X, wait=[xload])
            E("vector", "reduce_sum", s2c, gbuf, axis=AX.X, wait=[(AC, sq_done)])
            E("vector", "tensor_scalar_mul", stat_m[:, tau:tau + 1], s1c, 1.0 / D,
              incs=[inc(DV)])
            E("vector", "tensor_tensor", m1c, stat_m[:, tau:tau + 1],
              stat_m[:, tau:tau + 1], ALU.mult, wait=[(DV, DV.n)], incs=[inc(DV)])
            E("vector", "scalar_tensor_tensor", m2c, s2c, 1.0 / D, m1c,
              ALU.mult, ALU.subtract, wait=[(DV, DV.n)], incs=[inc(DV)])
            var_done = DV.n
            E("scalar", "activation", m2c, m2c, AF.Ln, bias=LN_EPS,
              wait=[(DV, var_done)])
            E("scalar", "activation", stat_r[:, tau:tau + 1], m2c, AF.Exp,
              scale=-0.5, incs=[inc(AC)])
            rstd_done = AC.n
            E("vector", "tensor_scalar", xnbf, xt[b], stat_m[:, tau:tau + 1],
              stat_r[:, tau:tau + 1], ALU.subtract, ALU.mult,
              wait=[(AC, rstd_done), (PE, tp_pe_prev)], incs=[inc(DV)])
            xa_dv[tau] = DV.n
            xt_guard[b] = DV.n
            xn_done = DV.n
            ptb = pt[tau % 2]
            for ki in range(NKI):
                E("tensor", "transpose", ptb[:, ki * 128:(ki + 1) * 128],
                  xnbf[:, ki * 128:(ki + 1) * 128], idbf_sb,
                  wait=([(DV, xn_done), (AC, pt_ac[tau % 2])] if ki == 0 else []),
                  incs=([inc(PE)] if ki == NKI - 1 else []))
            tp_pe_prev = PE.n
            for ki in range(NKI):
                w = []
                if ki == 0:
                    w = [(PE, tp_pe_prev)]
                    if seq > 0 and tau == 0:
                        w.append((PE, PE.n))
                E("scalar", "copy",
                  xnT[:, ki * S + tau * 128: ki * S + (tau + 1) * 128],
                  ptb[:, ki * 128:(ki + 1) * 128],
                  wait=w, incs=([inc(AC)] if ki == NKI - 1 else []))
            pt_ac[tau % 2] = AC.n
            xnT_ac[tau] = AC.n
        for tau in range(NT):
            b = tau % 2
            bank = pb[2 + tau % 2]
            for ki in range(NKI):
                E("tensor", "matmul", bank[:, 0:8],
                  xnT[:, ki * S + tau * 128: ki * S + (tau + 1) * 128],
                  gate_sb[:, ki * 8:(ki + 1) * 8],
                  start=(ki == 0), stop=(ki == NKI - 1),
                  wait=([(AC, max(xnT_ac[tau], gate_ac[tau % 2])),
                         (DV, gate_dv[tau % 2])] if ki == 0 else []),
                  incs=([inc(PE)] if ki == NKI - 1 else []))
            g_done = PE.n
            E("vector", "reduce_max", m1c, bank[:, 0:8], axis=AX.X, negate=True,
              wait=[(PE, g_done)], incs=[inc(DV)])
            E("scalar", "activation", tiny, bank[:, 0:8], AF.Exp, bias=m1c,
              wait=[(DV, DV.n)], incs=[inc(AC)])
            gate_ac[tau % 2] = AC.n
            for meth, args in [
                ("reduce_sum", (s1c, tiny)),
                ("reciprocal", (s2c, s1c)),
                ("tensor_scalar_mul", (tiny, tiny, s2c)),
                ("tensor_add", (tiny2, tiny, gateb_b)),
                ("reduce_max", (m1c, tiny2)),
                ("tensor_single_scalar", (tiny3, tiny2, m1c, ALU.is_equal)),
                ("scalar_tensor_tensor", (tiny3, tiny3, -1e30, tiny2,
                                          ALU.mult, ALU.add)),
                ("reduce_max", (m2c, tiny3)),
                ("tensor_single_scalar", (tiny2, tiny2, m2c, ALU.is_ge)),
                ("tensor_mul", (rw_sb[:, tau * 8:(tau + 1) * 8], tiny2, tiny)),
                ("tensor_copy", (rwx[:, 0:8], rw_sb[:, tau * 8:(tau + 1) * 8])),
            ]:
                kwx = {"axis": AX.X} if meth.startswith("reduce") else {}
                w = [(DV, DV.n)]
                if meth == "reduce_sum":
                    w.append((AC, AC.n))
                E("vector", meth, *args, wait=w, incs=[inc(DV)], **kwx)
            E("vector", "memset", rwx[:, 9:16], 0.0)
            E("vector", "memset", rwx[:, 8:9], 1.0, incs=[inc(DV)])
            rwx_done = DV.n
            gate_dv[tau % 2] = rwx_done
            ptb2 = pt[1 - tau % 2]
            E("tensor", "transpose", ptb2[0:16, 0:128], rwx, idbf_sb,
              wait=[(DV, rwx_done), (AC, pt_ac[1 - tau % 2])], incs=[inc(PE)])
            E("scalar", "copy", rwT[:, tau * 128:(tau + 1) * 128],
              ptb2[0:9, 0:128], wait=[(PE, PE.n)], incs=[inc(AC)])
            pt_ac[1 - tau % 2] = AC.n
            for ncol in range(2):
                i2 = (tau * 2 + ncol) % 2
                vbank = pb[i2]
                for ki in range(NKI):
                    w = []
                    if ki == 0:
                        w = wv_done + [(DV, v_dv[i2]), (AC, xnT_ac[tau])]
                    E("tensor", "matmul", vbank[:, 0:512],
                      xnT[:, ki * S + tau * 128: ki * S + (tau + 1) * 128],
                      wvsb[:, ki * 1024 + ncol * 512: ki * 1024 + (ncol + 1) * 512],
                      start=(ki == 0), stop=False, wait=w)
                E("tensor", "matmul", vbank[:, 0:512], onesrow_sb,
                  bvr_st[:, ncol * 512:(ncol + 1) * 512],
                  start=False, stop=True, incs=[inc(PE)])
                w = [(PE, PE.n)]
                if seq > 0 and tau == 0 and ncol == 0:
                    w.append((AC, AC.n))
                E("vector", "tensor_copy",
                  v_nat[:, tau * 1024 + ncol * 512: tau * 1024 + (ncol + 1) * 512],
                  vbank[:, 0:512], wait=w, incs=[inc(DV)])
                v_dv[i2] = DV.n
                vmark[tau] = DV.n

        proj_ac = {0: 0, 1: 0}
        dst_dv = {0: 0, 1: 0}
        z_ac = {0: 0, 1: 0}
        pv_pe = {}
        sq_ac = {}
        pair_pe = {0: 0, 1: 0}
        QKBUF = {0: (q1b, sqb, t1buf, t2buf, rstd_f, rstd_b),
                 1: (q1b_k, sqb_k, t1buf_k, t2buf_k, rstd_f_k, rstd_b_k)}
        SBANK = {(0, 0): pb[2], (0, 1): pb[2], (1, 0): pb[3], (1, 1): pb[3]}
        wqk_tok = {}
        wqsl = wqs_h.rearrange("p k (pp o) -> p k pp o", pp=NPAIR)
        wksl = wks_h.rearrange("p k (pp o) -> p k pp o", pp=NPAIR)
        for pair in range(2):
            w0 = [(PE, PE.n)] if pair == 0 else []
            if seq == 0 and pair == 0:
                w0 = w0 + wq_cast + wk_cast
            wqk_tok[pair] = wmax([
                QS.dma(wqp2[pair % 2], wqsl[:, :, pair, :], wait=w0),
                QS.dma(wkp2[pair % 2], wksl[:, :, pair, :])])
        for pair in range(NPAIR):
            wqk_done = wqk_tok[pair]
            wqp, wkp = wqp2[pair % 2], wkp2[pair % 2]
            kadd = {0: 0, 1: 0}
            for c in range(4):
                pdone = {}
                for qk in range(2):
                    wsel = wqp if qk == 0 else wkp
                    for ki in range(NKI):
                        w = []
                        if ki == 0:
                            w = wqk_done + [(AC, max(proj_ac[qk], xnT_ac[c * 4 + 3],
                                                     sq_ac.get((qk, 1), 0))),
                                            (DV, max(dst_dv[qk],
                                                     vmark[NT - 1] if (pair, c) == (0, 0) else 0))]
                        E("tensor", "matmul", pb[qk][:, 0:512],
                          wsel[:, ki * 128:(ki + 1) * 128],
                          xnT[:, ki * S + c * 512: ki * S + (c + 1) * 512],
                          start=(ki == 0), stop=(ki == NKI - 1), wait=w,
                          incs=([inc(PE)] if ki == NKI - 1 else []))
                    pdone[qk] = PE.n
                if c == 3 and pair + 2 < NPAIR:
                    p2 = pair + 2
                    wqk_tok[p2] = wmax([
                        QS.dma(wqp2[p2 % 2], wqsl[:, :, p2, :],
                               wait=[(PE, pdone[1])]),
                        QS.dma(wkp2[p2 % 2], wksl[:, :, p2, :])])
                st = {}
                for qk in range(2):
                    bsel = bqt_sb if qk == 0 else bkt_sb
                    _q1 = QKBUF[qk][0]
                    E("scalar", "activation", _q1, pb[qk][:, 0:512], AF.Identity,
                      bias=bsel[:, pair:pair + 1],
                      wait=[(PE, pdone[qk]), (DV, dst_dv[qk])])
                for qk in range(2):
                    bsel = bqt_sb if qk == 0 else bkt_sb
                    _sq = QKBUF[qk][1]
                    E("scalar", "activation", _sq, pb[qk][:, 0:512], AF.Square,
                      bias=bsel[:, pair:pair + 1], incs=[inc(AC)])
                    proj_ac[qk] = AC.n
                for qk in range(2):
                    E("tensor", "matmul", pb[qk][0:2, 0:512], ones2_sb,
                      QKBUF[qk][1], start=True, stop=True,
                      wait=[(AC, proj_ac[qk])], incs=[inc(PE)])
                    st[qk] = PE.n
                for qk in range(2):
                    E("scalar", "activation", QKBUF[qk][4], pb[qk][0:2, 0:512],
                      AF.Ln, scale=1.0 / DH, bias=RMS_EPS, wait=[(PE, st[qk])])
                for qk in range(2):
                    E("scalar", "activation", QKBUF[qk][4], QKBUF[qk][4], AF.Exp,
                      scale=-0.5, bias=(-math.log(DH) if qk == 0 else 0.0),
                      incs=[inc(AC)])
                    st[qk] = AC.n
                for qk in range(2):
                    E("vector", "tensor_copy", QKBUF[qk][5], QKBUF[qk][4],
                      wait=[(AC, st[qk])], incs=[inc(DV)])
                    st[qk] = DV.n
                for qk in range(2):
                    E("tensor", "matmul", pb[qk][:, 0:512], sel2_sb, QKBUF[qk][5],
                      start=True, stop=True, wait=[(DV, st[qk])], incs=[inc(PE)])
                    st[qk] = PE.n
                for qk in range(2):
                    E("vector", "tensor_tensor", QKBUF[qk][1], QKBUF[qk][0],
                      pb[qk][:, 0:512], ALU.mult, wait=[(PE, st[qk])],
                      incs=[inc(DV)])
                    st[qk] = DV.n
                for qk in range(2):
                    E("tensor", "matmul", pb[qk][:, 0:512], prot_sb, QKBUF[qk][1],
                      start=True, stop=True, wait=[(DV, st[qk])], incs=[inc(PE)])
                    st[qk] = PE.n
                    if qk == 1 and c == 3:
                        pair_pe[pair % 2] = PE.n
                for qk in range(2):
                    E("vector", "tensor_tensor", QKBUF[qk][3], pb[qk][:, 0:512],
                      sw[:, c * 512:(c + 1) * 512], ALU.mult, wait=[(PE, st[qk])])
                for qk in range(2):
                    E("vector", "tensor_tensor", QKBUF[qk][2], QKBUF[qk][1],
                      cw[:, c * 512:(c + 1) * 512], ALU.mult)
                for qk in range(2):
                    dst = qTp if qk == 0 else kTp
                    E("vector", "tensor_add", dst[:, c * 512:(c + 1) * 512],
                      QKBUF[qk][2], QKBUF[qk][3], incs=[inc(DV)])
                    dst_dv[qk] = DV.n
                    kadd[qk] = DV.n
                qc = c
                nkt = qc * 4 + 4
                for kt in range(nkt):
                    patt = {}
                    for hh in range(2):
                        sbank = SBANK[(hh, kt % 2)]
                        pbf = pbuf[2 * hh + (kt % 2)]
                        w = [(AC, sq_ac.get((hh, kt % 2), 0))]
                        if kt <= 1:
                            w.append((DV, max(kadd[0], kadd[1])))
                        E("tensor", "matmul", sbank[:, 0:512],
                          kTp[64 * hh:64 * hh + 64, kt * 128:(kt + 1) * 128],
                          qTp[64 * hh:64 * hh + 64, qc * 512:(qc + 1) * 512],
                          start=True, stop=True, tile_position=(64 * hh, 0),
                          skip_group_check=True, wait=w, incs=[inc(PE)])
                        s_done = PE.n
                        wsq = [(PE, s_done), (PE, pv_pe.get((hh, kt % 2), 0))]
                        E("scalar", "activation", pbf, sbank[:, 0:512],
                          AF.Square, wait=wsq, incs=[inc(AC)])
                        sq_ac[(hh, kt % 2)] = AC.n
                        if kt >= qc * 4:
                            off = (kt - qc * 4) * 128
                            if off > 0:
                                E("vector", "memset", pbf[:, 0:off], 0.0,
                                  wait=[(AC, AC.n)])
                            E("vector", "tensor_tensor", pbf[:, off:off + 128],
                              pbf[:, off:off + 128], tri_sb, ALU.mult,
                              wait=([(AC, AC.n)] if off == 0 else []),
                              incs=[inc(DV)])
                            patt[hh] = (DV, DV.n)
                        else:
                            patt[hh] = (AC, AC.n)
                    for hh in range(2):
                        zbank = pb[4 + hh]
                        pbf = pbuf[2 * hh + (kt % 2)]
                        w = [patt[hh], (DV, vmark[kt])]
                        if kt == 0:
                            w.append((AC, z_ac[hh]))
                        E("tensor", "matmul", zbank[0:64, 0:512],
                          v_nat[:, kt * 1024 + (2 * pair + hh) * 64:
                                kt * 1024 + (2 * pair + hh) * 64 + 64],
                          pbf, start=(kt == 0), stop=(kt == nkt - 1),
                          skip_group_check=True, wait=w, incs=[inc(PE)])
                        pv_pe[(hh, kt % 2)] = PE.n
                for hh in range(2):
                    E("scalar", "copy",
                      zT[64 * hh:64 * hh + 64,
                         pair * S + qc * 512: pair * S + (qc + 1) * 512],
                      pb[4 + hh][0:64, 0:512],
                      wait=[(PE, pv_pe[(hh, (nkt - 1) % 2)])], incs=[inc(AC)])
                    z_ac[hh] = AC.n
        barrier()

        for half in range(2):
            t0 = half * HALF
            yinit_dv = {0: 0, 1: 0}
            for tl in range(NTL):
                tau = half * NTL + tl
                for ncol in range(2):
                    i2 = (tl * 2 + ncol) % 2
                    bank = pb[4 + i2]
                    E("tensor", "matmul", bank[:, 0:512],
                      rwT[:, tau * 128:(tau + 1) * 128],
                      bp_sb[:, ncol * 512:(ncol + 1) * 512],
                      start=True, stop=True,
                      wait=([(DV, yinit_dv[i2])] if yinit_dv[i2] else []),
                      incs=[inc(PE)])
                    E("vector", "tensor_copy",
                      y_acc[:, tl * 1024 + ncol * 512: tl * 1024 + (ncol + 1) * 512],
                      bank[:, 0:512], wait=[(PE, PE.n)], incs=[inc(DV)])
                    yinit_dv[i2] = DV.n
            oe_dv[0] = oe_dv[1] = DV.n
            for e in range(NE):
                silu_first = True
                for mo in range(NMO):
                    sb_i = mo % 2
                    w = [(PE, slab_g[sb_i])]
                    if seq == 0 and half == 0 and mo == 0:
                        w = w + wcast_tok[e]
                    q1q = QS if (seq == 0 and half == 0) else QW
                    t1_ = q1q.dma(w1m[sb_i], w1s_h[e, :, mo * 1024:(mo + 1) * 1024],
                                  wait=w)
                    t3_ = QS.dma(w3m[sb_i], w3s_h[e, :, mo * 1024:(mo + 1) * 1024],
                                 wait=w)
                    slab_dma = wmax([t1_, t3_])
                    for ncol in range(2):
                        i2 = (mo * 2 + ncol) % 2
                        g1bank, g3bank = pb[0 + i2], pb[2 + i2]
                        for ki in range(NKI):
                            w = []
                            if ki == 0:
                                w = slab_dma + [(DV, h_dv[i2])]
                            E("tensor", "matmul", g1bank[:, 0:512],
                              w1m[sb_i][:, ki * 128:(ki + 1) * 128],
                              xnT[:, ki * S + t0 + ncol * 512:
                                  ki * S + t0 + (ncol + 1) * 512],
                              start=(ki == 0), stop=(ki == NKI - 1), wait=w)
                        for ki in range(NKI):
                            E("tensor", "matmul", g3bank[:, 0:512],
                              w3m[sb_i][:, ki * 128:(ki + 1) * 128],
                              xnT[:, ki * S + t0 + ncol * 512:
                                  ki * S + t0 + (ncol + 1) * 512],
                              start=(ki == 0), stop=(ki == NKI - 1),
                              incs=([inc(PE)] if ki == NKI - 1 else []))
                        gg_done = PE.n
                        if ncol == 1:
                            slab_g[sb_i] = gg_done
                        E("vector", "tensor_scalar_add", t3b, g3bank[:, 0:512],
                          b3t_sb[:, e * NMO + mo: e * NMO + mo + 1],
                          wait=[(PE, gg_done), (AC, h_ac[i2])], incs=[inc(DV)])
                        E("vector", "scalar_tensor_tensor", prodb[i2],
                          g1bank[:, 0:512],
                          b1t_sb[:, e * NMO + mo: e * NMO + mo + 1], t3b,
                          ALU.add, ALU.mult, wait=[(DV, DV.n)], incs=[inc(DV)])
                        h_dv[i2] = DV.n
                        wsilu = [(DV, DV.n)]
                        if silu_first:
                            wsilu.append((PE, oe_last_pe))
                            wsilu += QS.drain_tokens()
                            silu_first = False
                        E("scalar", "activation",
                          hT[:, mo * 1024 + ncol * 512: mo * 1024 + (ncol + 1) * 512],
                          prodb[i2], AF.Silu, wait=wsilu, incs=[inc(AC)])
                        h_ac[i2] = AC.n
                h_done_ac = AC.n
                w2_toks = []
                for hi in range(2):
                    off = 0 if hi == 0 else 6 * 1024
                    nel = 6 * 1024 if hi == 0 else 5 * 1024
                    w = []
                    if hi == 0:
                        w = [(PE, oe_last_pe)]
                        if seq == 0 and half == 0:
                            w = w + wcast_tok[e]
                    w2_toks.append(QS.dma(w2sb[:, off:off + nel],
                                          w2s_h[e, :, off:off + nel], wait=w))
                w2_dma = wmax(w2_toks)
                for tl in range(NTL):
                    tau = half * NTL + tl
                    for ncol in range(2):
                        i2 = (tl * 2 + ncol) % 2
                        bank = pb[4 + i2]
                        for kh in range(NMO):
                            w = []
                            if kh == 0:
                                w = w2_dma + [(AC, h_done_ac), (DV, oe_dv[i2])]
                            E("tensor", "matmul", bank[:, 0:512],
                              hT[:, kh * 1024 + tl * 128: kh * 1024 + (tl + 1) * 128],
                              w2sb[:, kh * 1024 + ncol * 512:
                                   kh * 1024 + (ncol + 1) * 512],
                              start=(kh == 0), stop=(kh == NMO - 1), wait=w,
                              incs=([inc(PE)] if kh == NMO - 1 else []))
                        oe_last_pe = PE.n
                        ysl = y_acc[:, tl * 1024 + ncol * 512:
                                    tl * 1024 + (ncol + 1) * 512]
                        if e < 8:
                            E("vector", "scalar_tensor_tensor", ysl, bank[:, 0:512],
                              rw_sb[:, tau * 8 + e: tau * 8 + e + 1], ysl,
                              ALU.mult, ALU.add, wait=[(PE, oe_last_pe)],
                              incs=[inc(DV)])
                        else:
                            E("vector", "tensor_add", ysl, ysl, bank[:, 0:512],
                              wait=[(PE, oe_last_pe)], incs=[inc(DV)])
                        oe_dv[i2] = DV.n
            barrier()
            wo_toks = [QW.dma(wosb, wos_h,
                              wait=[(PE, oe_last_pe)]
                              + (wo_cast if seq == 0 and half == 0 else []))]
            wo_dma = wmax(wo_toks)
            fin_dv = {0: 0, 1: 0}
            fin_st = {0: [], 1: []}
            fin_nc = {0: 0, 1: 0}
            last_fin_dv = 0
            for tl in range(NTL):
                tau = half * NTL + tl
                b = tl % 2
                xload = QX.dma(xt[b], x_h[seq, tau * 128:(tau + 1) * 128, :],
                               wait=([(DV, xt_guard[b])] if xt_guard[b] else []))
                E("scalar", "activation", gbuf, y_acc[:, tl * 1024:(tl + 1) * 1024],
                  AF.Silu, wait=[(DV, last_fin_dv)], incs=[inc(AC)])
                g_done = AC.n
                swo_done = {}
                for ncol in range(2):
                    i2 = (tl * 2 + ncol) % 2
                    bank = pb[4 + i2]
                    for pp in range(NKI):
                        w = []
                        if pp == 0:
                            w = wo_dma + [(DV, fin_nc[i2] or oe_dv[i2])]
                        E("tensor", "matmul", bank[:, 0:512],
                          zT[:, pp * S + tau * 128: pp * S + (tau + 1) * 128],
                          wosb[:, pp * 1024 + ncol * 512: pp * 1024 + (ncol + 1) * 512],
                          start=(pp == 0), stop=(pp == NKI - 1), wait=w,
                          incs=([inc(PE)] if pp == NKI - 1 else []))
                    swo_done[ncol] = PE.n
                for ncol in range(2):
                    E("vector", "tensor_scalar",
                      sbuf_s[:, ncol * 512:(ncol + 1) * 512],
                      xt[b][:, ncol * 512:(ncol + 1) * 512],
                      stat_m[:, tau:tau + 1], stat_r[:, tau:tau + 1],
                      ALU.subtract, ALU.mult, wait=([xload] if ncol == 0 else []))
                for ncol in range(2):
                    E("vector", "tensor_add", sbuf_s[:, ncol * 512:(ncol + 1) * 512],
                      sbuf_s[:, ncol * 512:(ncol + 1) * 512],
                      pb[4 + (tl * 2 + ncol) % 2][:, 0:512],
                      wait=[(PE, swo_done[ncol])],
                      incs=([inc(DV)] if ncol == 1 else []))
                fin_nc[tl * 2 % 2] = DV.n
                fin_nc[(tl * 2 + 1) % 2] = DV.n
                for ncol in range(2):
                    E("vector", "tensor_mul", outb[b][:, ncol * 512:(ncol + 1) * 512],
                      sbuf_s[:, ncol * 512:(ncol + 1) * 512],
                      gbuf[:, ncol * 512:(ncol + 1) * 512],
                      wait=([(AC, g_done)] + fin_st[b] if ncol == 0 else []))
                for ncol in range(2):
                    E("vector", "tensor_add", outb[b][:, ncol * 512:(ncol + 1) * 512],
                      outb[b][:, ncol * 512:(ncol + 1) * 512],
                      xt[b][:, ncol * 512:(ncol + 1) * 512],
                      incs=([inc(DV)] if ncol == 1 else []))
                fin_dv[b] = DV.n
                xt_guard[b] = DV.n
                last_fin_dv = DV.n
                fin_st[b] = [QO.dma(out_h[seq, tau * 128:(tau + 1) * 128, :],
                                    outb[b], wait=[(DV, fin_dv[b])])]
            barrier()

    if DEBUG:
        barrier()
        for nm, buf in [("d_cw", cw), ("d_sw", sw), ("d_statm", stat_m),
                        ("d_statr", stat_r), ("d_xnT", xnT), ("d_zT", zT),
                        ("d_rw", rw_sb), ("d_rwT", rwT), ("d_y", y_acc),
                        ("d_gate", gate_sb)]:
            QO.dma(dbg[nm], buf)
    E("sync", "nop", wait=QO.drain_tokens() + QS.drain_tokens())

    with nc.Block() as block:
        block.sync(lambda e: p.replay(e, "sync"))
        block.gpsimd(lambda e: p.replay(e, "gpsimd"))
        block.tensor(lambda e: p.replay(e, "tensor"))
        block.vector(lambda e: p.replay(e, "vector"))
        block.scalar(lambda e: p.replay(e, "scalar"))
    return nc


def _layout_inputs(i, **inp):
    f = lambda a: np.asarray(a, np.float32)
    nw = f(inp["L0_attn_norm_w"])
    nws = np.ascontiguousarray(nw.reshape(32, 2)[:, ::-1].reshape(64))
    c = np.ascontiguousarray
    wslab = lambda w: c(f(w).T.reshape(NKI, 128, D).transpose(1, 0, 2))
    return {
        "x": c(f(inp["L0_attn_x"])[2 * i:2 * i + 2]),
        "wqt": wslab(inp["L0_attn_Wq"]),
        "wkt": wslab(inp["L0_attn_Wk"]),
        "wvt": wslab(inp["L0_attn_Wv"]),
        "wot": wslab(inp["L0_attn_Wo"]),
        "bqt": c(f(inp["L0_attn_bq"]).reshape(8, 128).T),
        "bkt": c(f(inp["L0_attn_bk"]).reshape(8, 128).T),
        "bvr": c(f(inp["L0_attn_bv"]).reshape(1, D)),
        "nw": c(np.tile(nw, 2).reshape(128, 1)),
        "nws": c(np.tile(nws, 2).reshape(128, 1)),
        "gwt": c(f(inp["L0_ffn_gate_w"]).T.reshape(NKI, 128, 8).transpose(1, 0, 2).reshape(128, NKI * 8)),
        "gb": c(f(inp["L0_ffn_gate_b"]).reshape(1, 8)),
        "b1t": c(np.concatenate(
            [f(inp["L0_ffn_b1"]).reshape(8, NMO, 128).transpose(0, 2, 1),
             f(inp["L0_ffn_sb1"]).reshape(NMO, 128).T[None]], 0)
            .transpose(1, 0, 2).reshape(128, NE * NMO)),
        "b3t": c(np.concatenate(
            [f(inp["L0_ffn_b3"]).reshape(8, NMO, 128).transpose(0, 2, 1),
             f(inp["L0_ffn_sb3"]).reshape(NMO, 128).T[None]], 0)
            .transpose(1, 0, 2).reshape(128, NE * NMO)),
        "bp": c(np.concatenate(
            [f(inp["L0_ffn_b2"]), f(inp["L0_ffn_sb2"]).reshape(1, D)], 0)),
        "w1": c(f(inp["L0_ffn_w1"]).reshape(8, NKI, 128, H).transpose(0, 2, 1, 3)),
        "w3": c(f(inp["L0_ffn_w3"]).reshape(8, NKI, 128, H).transpose(0, 2, 1, 3)),
        "w2": c(f(inp["L0_ffn_w2"]).reshape(8, NMO, 128, D).transpose(0, 2, 1, 3)),
        "sw1": c(f(inp["L0_ffn_sw1"]).reshape(NKI, 128, H).transpose(1, 0, 2)),
        "sw3": c(f(inp["L0_ffn_sw3"]).reshape(NKI, 128, H).transpose(1, 0, 2)),
        "sw2": c(f(inp["L0_ffn_sw2"]).reshape(NMO, 128, D).transpose(1, 0, 2)),
    }


_NC_CACHE = None


def kernel(**inputs):
    global _NC_CACHE
    if _NC_CACHE is None:
        _NC_CACHE = build_nc()
    in_maps = [_layout_inputs(i, **inputs) for i in range(NCORE)]
    res = run_bass_kernel_spmd(_NC_CACHE, in_maps, core_ids=list(range(NCORE)))
    if DEBUG:
        import pickle
        with open("/root/problem/_dbg_dumps.pkl", "wb") as fh:
            pickle.dump([{k: np.asarray(v) for k, v in r.items()} for r in res.results], fh)
    return np.concatenate([res.results[i]["out"] for i in range(NCORE)], axis=0)


if __name__ == "__main__":
    build_nc()
    print("built ok")


# version 23
# speedup vs baseline: 1.1990x; 1.0046x over previous
import math
import numpy as np
import ml_dtypes
import concourse.bass as bass
import concourse.mybir as mybir
from concourse.bass_utils import run_bass_kernel_spmd

F32 = mybir.dt.float32
BF16 = mybir.dt.bfloat16
U8 = mybir.dt.uint8
AF = mybir.ActivationFunctionType
ALU = mybir.AluOpType
AX = mybir.AxisListType

B, S, D = 16, 2048, 1024
NCORE = 8
BLOC = B // NCORE
NH, DH, NPAIR = 16, 64, 8
NT = S // 128
H = 1408
NMO = H // 128
NE = 9
NKI = D // 128
HALF = 1024
NTL = HALF // 128
LN_EPS = 1e-5
RMS_EPS = float(np.finfo(np.float32).eps)

DEBUG = False
ENGS = ("sync", "gpsimd", "tensor", "vector", "scalar")


class Sem:
    __slots__ = ("h", "n")

    def __init__(self, h):
        self.h = h
        self.n = 0


class Prog:
    def __init__(self):
        self.ops = {e: [] for e in ENGS}

    def emit(self, eng, meth, *args, wait=None, incs=None, **kw):
        self.ops[eng].append((meth, args, kw, list(wait or []), list(incs or [])))

    def replay(self, eng_obj, name):
        for meth, args, kw, waits, incs in self.ops[name]:
            for s, v in waits:
                if v > 0:
                    eng_obj.wait_ge(s.h, v)
            r = getattr(eng_obj, meth)(*args, **kw)
            for s, k in incs:
                r.then_inc(s.h, k)


def _rope_tables():
    inv = (1.0 / (10000.0 ** (np.arange(0, DH, 2, dtype=np.float32) / DH))).astype(np.float32)
    t = np.arange(S, dtype=np.float32)
    ang = (inv[:, None] * t[None, :]).astype(np.float32)
    cos = np.cos(ang).astype(np.float32)
    sin = np.sin(ang).astype(np.float32)
    cosT = np.concatenate([cos, cos], 0)
    sinT = np.concatenate([sin, sin], 0)
    return np.tile(cosT, (2, 1)), np.tile(sinT, (2, 1))


COS128, SIN128 = _rope_tables()
ID_BF = np.eye(128, dtype=ml_dtypes.bfloat16)
PROT = np.zeros((128, 128), dtype=ml_dtypes.bfloat16)
for _i in range(64):
    PROT[2 * _i + 1, 2 * _i] = -1.0
    PROT[2 * _i, 2 * _i + 1] = 1.0
TRI = np.triu(np.ones((128, 128), dtype=np.float32)).astype(ml_dtypes.bfloat16)
ONES2 = np.zeros((128, 2), dtype=ml_dtypes.bfloat16)
ONES2[0:64, 0] = 1.0
ONES2[64:128, 1] = 1.0
SEL2 = np.zeros((2, 128), dtype=ml_dtypes.bfloat16)
SEL2[0, 0:64] = 1.0
SEL2[1, 64:128] = 1.0
ONESROW = np.ones((1, 128), dtype=ml_dtypes.bfloat16)
CONSTVALS = [LN_EPS, RMS_EPS, -math.log(DH), 1.0 / DH, -0.5, 1.0 / 1024.0, -1e30]
CONSTNP = np.tile(np.array([CONSTVALS], dtype=np.float32), (128, 1))


def build_nc():
    nc = bass.Bass("TRN2", target_bir_lowering=False, detect_race_conditions=False)

    x_h = nc.dram_tensor("x", [BLOC, S, D], F32, kind="ExternalInput").ap()
    wqt_h = nc.dram_tensor("wqt", [128, NKI, D], F32, kind="ExternalInput").ap()
    wkt_h = nc.dram_tensor("wkt", [128, NKI, D], F32, kind="ExternalInput").ap()
    wvt_h = nc.dram_tensor("wvt", [128, NKI, D], F32, kind="ExternalInput").ap()
    wot_h = nc.dram_tensor("wot", [128, NKI, D], F32, kind="ExternalInput").ap()
    bqt_h = nc.dram_tensor("bqt", [128, NPAIR], F32, kind="ExternalInput").ap()
    bkt_h = nc.dram_tensor("bkt", [128, NPAIR], F32, kind="ExternalInput").ap()
    bvr_h = nc.dram_tensor("bvr", [1, D], F32, kind="ExternalInput").ap()
    nw_h = nc.dram_tensor("nw", [128, 1], F32, kind="ExternalInput").ap()
    nws_h = nc.dram_tensor("nws", [128, 1], F32, kind="ExternalInput").ap()
    gwt_h = nc.dram_tensor("gwt", [128, NKI * 8], F32, kind="ExternalInput").ap()
    gb_h = nc.dram_tensor("gb", [1, 8], F32, kind="ExternalInput").ap()
    b1t_h = nc.dram_tensor("b1t", [128, NE * NMO], F32, kind="ExternalInput").ap()
    b3t_h = nc.dram_tensor("b3t", [128, NE * NMO], F32, kind="ExternalInput").ap()
    bp_h = nc.dram_tensor("bp", [NE, D], F32, kind="ExternalInput").ap()
    w1_h = nc.dram_tensor("w1", [8, 128, NKI, H], F32, kind="ExternalInput").ap()
    w3_h = nc.dram_tensor("w3", [8, 128, NKI, H], F32, kind="ExternalInput").ap()
    w2_h = nc.dram_tensor("w2", [8, 128, NMO, D], F32, kind="ExternalInput").ap()
    sw1_h = nc.dram_tensor("sw1", [128, NKI, H], F32, kind="ExternalInput").ap()
    sw3_h = nc.dram_tensor("sw3", [128, NKI, H], F32, kind="ExternalInput").ap()
    sw2_h = nc.dram_tensor("sw2", [128, NMO, D], F32, kind="ExternalInput").ap()
    out_h = nc.dram_tensor("out", [BLOC, S, D], F32, kind="ExternalOutput").ap()
    dbg = {}
    if DEBUG:
        for nm, shp, dt in [("d_cw", [128, S], BF16), ("d_sw", [128, S], F32),
                            ("d_statm", [128, NT], F32), ("d_statr", [128, NT], F32),
                            ("d_xnT", [128, NKI * S], BF16), ("d_zT", [128, NPAIR * S], BF16),
                            ("d_rw", [128, NT * 8], F32), ("d_rwT", [9, NT * 128], BF16),
                            ("d_u1", [128, 32768], U8),
                            ("d_y", [128, 8192], F32), ("d_gate", [128, NKI * 8], BF16),
                            ("d_qT", [128, S], BF16), ("d_kT", [128, S], BF16),
                            ("d_v", [128, 16384], BF16)]:
            dbg[nm] = nc.dram_tensor(nm, shp, dt, kind="ExternalOutput").ap()

    cos_c = nc.inline_tensor(COS128, "cos128").ap()
    sin_c = nc.inline_tensor(SIN128, "sin128").ap()
    idbf_c = nc.inline_tensor(ID_BF, "idbf").ap()
    prot_c = nc.inline_tensor(PROT, "prot").ap()
    tri_c = nc.inline_tensor(TRI, "tri").ap()
    ones2_c = nc.inline_tensor(ONES2, "ones2").ap()
    sel2_c = nc.inline_tensor(SEL2, "sel2").ap()
    onesrow_c = nc.inline_tensor(ONESROW, "onesrow").ap()
    cvals_c = nc.inline_tensor(CONSTNP, "cvals").ap()

    w1s_h = nc.dram_tensor("w1s", [NE, 128, NMO * 1024], BF16, kind="Internal").ap()
    w3s_h = nc.dram_tensor("w3s", [NE, 128, NMO * 1024], BF16, kind="Internal").ap()
    w2s_h = nc.dram_tensor("w2s", [NE, 128, NMO * 1024], BF16, kind="Internal").ap()
    wos_h = nc.dram_tensor("wos", [128, NKI * 1024], BF16, kind="Internal").ap()
    wqs_h = nc.dram_tensor("wqs", [128, NKI, 1024], BF16, kind="Internal").ap()
    wks_h = nc.dram_tensor("wks", [128, NKI, 1024], BF16, kind="Internal").ap()
    wvs_h = nc.dram_tensor("wvs", [128, NKI, 1024], BF16, kind="Internal").ap()

    def A(nm, shp, dt=F32):
        return nc.alloc_sbuf_tensor(nm, shp, dt).ap()

    xnT = A("xnT", [128, NKI * S], BF16)
    zT = A("zT", [128, NPAIR * S], BF16)
    u1 = nc.alloc_sbuf_tensor("u1", [128, 32768], U8)
    v_nat = u1.ap().bitcast(BF16)
    y_acc = u1.ap().bitcast(F32)
    u2 = nc.alloc_sbuf_tensor("u2", [128, 59392], U8)
    u2b = u2.ap().bitcast(BF16)
    u2f = u2.ap().bitcast(F32)

    def cb(off, n):
        assert off % 2 == 0
        return u2b[:, off // 2: off // 2 + n]

    def cf(off, n):
        assert off % 4 == 0
        return u2f[:, off // 4: off // 4 + n]

    hT = cb(0, NMO * 1024)
    w2sb = cb(22528, NMO * 1024)
    w1m = [cb(45056 + i * 2048, 1024) for i in range(2)]
    w3m = [cb(49152 + i * 2048, 1024) for i in range(2)]
    t3b = cf(53248, 512)
    prodb = [cf(55296 + i * 2048, 512) for i in range(2)]
    qTp = cb(0, S)
    kTp = cb(4096, S)
    wqp2 = [cb(8192 + i * 2048, NKI * 128) for i in range(2)]
    wvsb = cb(12288, NKI * 1024)
    wkp2 = [cb(12288 + i * 2048, NKI * 128) for i in range(2)]
    pbuf = [cb(28672 + i * 1024, 512) for i in range(4)]
    q1b = cf(32768, 512)
    t1buf = cb(34816, 512)
    t2buf = cb(35840, 512)
    sqb = cb(36864, 512)
    rstd_f = cf(37888, 512)[0:2, :]
    rstd_b = cb(39936, 512)[0:2, :]
    cosst = cf(0, 2048)
    bvr_sb = cb(40960, 1024)[0:1, :]
    q1b_k = cf(16384, 512)
    sqb_k = cb(18432, 512)
    t1buf_k = cb(19456, 512)
    t2buf_k = cb(20480, 512)
    rstd_f_k = cf(21504, 512)[0:2, :]
    rstd_b_k = cb(23552, 512)[0:2, :]
    wosb = cb(0, NKI * 1024)

    wstage = [A(f"wstage{i}", [128, 1024], BF16) for i in range(2)]
    bvr_st = A("bvr_st", [1, D], BF16)
    cw = A("cw", [128, S], BF16)
    sw = A("sw", [128, S], F32)
    xt = [A(f"xt{i}", [128, D], F32) for i in range(2)]
    outb = [A(f"outb{i}", [128, D], F32) for i in range(2)]
    gbuf = A("gbuf", [128, D], F32)
    sbuf_s = A("sbufs", [128, D], F32)
    stat_m = A("stat_m", [128, NT], F32)
    stat_r = A("stat_r", [128, NT], F32)
    rw_sb = A("rw_sb", [128, NT * 8], F32)
    rwT = A("rwT", [9, NT * 128], BF16)
    rwx = A("rwx", [128, 16], BF16)
    gate_sb = A("gate_sb", [128, NKI * 8], BF16)
    gateb_b = A("gateb_b", [128, 8], F32)
    gb_sb = A("gb_sb", [1, 8], BF16)
    bqt_sb = A("bqt_sb", [128, NPAIR], F32)
    bkt_sb = A("bkt_sb", [128, NPAIR], F32)
    nw_sb = A("nw_sb", [128, 1], F32)
    nws_sb = A("nws_sb", [128, 1], F32)
    b1t_sb = A("b1t_sb", [128, NE * NMO], F32)
    b3t_sb = A("b3t_sb", [128, NE * NMO], F32)
    bp_sb = A("bp_sb", [NE, D], BF16)
    idbf_sb = A("idbf_sb", [128, 128], BF16)
    prot_sb = A("prot_sb", [128, 128], BF16)
    tri_sb = A("tri_sb", [128, 128], BF16)
    ones2_sb = A("ones2_sb", [128, 2], BF16)
    sel2_sb = A("sel2_sb", [2, 128], BF16)
    onesrow_sb = A("onesrow_sb", [1, 128], BF16)
    tiny = A("tiny", [128, 8], F32)
    tiny2 = A("tiny2", [128, 8], F32)
    tiny3 = A("tiny3", [128, 8], F32)
    cst_sb = A("cst_sb", [128, len(CONSTVALS)], F32)
    for _ci, _cv in enumerate(CONSTVALS):
        nc.const_aps.register(F32, float(_cv), cst_sb[:, _ci:_ci + 1])
    m1c = A("m1c", [128, 1], F32)
    m2c = A("m2c", [128, 1], F32)
    s1c = A("s1c", [128, 1], F32)
    s2c = A("s2c", [128, 1], F32)

    pb = [nc.alloc_psum_tensor(f"pb{i}", [128, 512], F32).ap() for i in range(6)]
    pt = [nc.alloc_psum_tensor(f"pt{i}", [128, 1024], BF16).ap() for i in range(2)]

    PE = Sem(nc.alloc_semaphore("PE"))
    DV = Sem(nc.alloc_semaphore("DV"))
    AC = Sem(nc.alloc_semaphore("AC"))
    BS = {e: Sem(nc.alloc_semaphore(f"BS_{e}")) for e in ENGS}

    def inc(s, k=1):
        s.n += k
        return (s, k)

    p = Prog()
    E = p.emit

    class Dq:

        def __init__(self, name, eng, nslots):
            self.eng = eng
            self.slots = [Sem(nc.alloc_semaphore(f"{name}{i}")) for i in range(nslots)]
            self.i = 0

        def dma(self, out, in_, wait=None, **kw):
            s = self.slots[self.i % len(self.slots)]
            self.i += 1
            w = list(wait or [])
            if s.n:
                w.append((s, s.n))
            s.n += 16
            E(self.eng, "dma_start", out=out, in_=in_, wait=w,
              incs=[(s, 16)], **kw)
            return (s, s.n)

        def drain_tokens(self):
            return [(s, s.n) for s in self.slots if s.n]

    QX = Dq("QX", "sync", 4)
    QS = Dq("QS", "sync", 3)
    QW = Dq("QW", "gpsimd", 6)
    QO = Dq("QO", "sync", 3)

    def wmax(tokens):
        best = {}
        for s, v in tokens:
            if v > best.get(id(s), (None, -1))[1]:
                best[id(s)] = (s, v)
        return list(best.values())

    BENGS = ("sync", "tensor", "vector", "scalar")

    def barrier():
        for eng in BENGS:
            E(eng, "nop", incs=[inc(BS[eng], 1)])
        vals = {eng: BS[eng].n for eng in BENGS}
        for eng in BENGS:
            E(eng, "nop", wait=[(BS[o], vals[o]) for o in BENGS if o != eng])

    setup_toks = []
    for dst, srcap in [(bqt_sb, bqt_h), (bkt_sb, bkt_h), (nw_sb, nw_h),
                       (nws_sb, nws_h), (b1t_sb, b1t_h), (b3t_sb, b3t_h),
                       (idbf_sb, idbf_c), (prot_sb, prot_c), (tri_sb, tri_c),
                       (ones2_sb, ones2_c), (sel2_sb, sel2_c),
                       (onesrow_sb, onesrow_c), (cst_sb, cvals_c)]:
        setup_toks.append(QX.dma(dst, srcap))
    setup_toks.append(QW.dma(bp_sb, bp_h))
    setup_toks.append(QW.dma(gate_sb, gwt_h))
    setup_toks.append(QW.dma(gb_sb, gb_h))
    setup_toks.append(QW.dma(bvr_st, bvr_h))
    ctok = QX.dma(cosst, cos_c)
    E("vector", "tensor_scalar_mul", cw, cosst, nw_sb,
      wait=wmax(setup_toks + [ctok]), incs=[inc(DV)])
    ctok = QX.dma(cosst, sin_c, wait=[(DV, DV.n)])
    E("vector", "tensor_scalar_mul", sw, cosst, nws_sb, wait=[ctok],
      incs=[inc(DV)])
    wv0_toks = [QW.dma(wvsb[:, ki * 1024:(ki + 1) * 1024], wvt_h[:, ki, :])
                for ki in range(NKI)]
    _wsi = [0]

    def _cast_slab(dst, srcap):
        i = _wsi[0] % 2
        _wsi[0] += 1
        t = QW.dma(wstage[i], srcap)
        return QW.dma(dst, wstage[i], wait=[t])

    wv_cast = wmax([_cast_slab(wvs_h[:, ki, :], wvt_h[:, ki, :])
                    for ki in range(NKI)])
    wq_cast = wmax([_cast_slab(wqs_h[:, ki, :], wqt_h[:, ki, :])
                    for ki in range(NKI)])
    wk_cast = wmax([_cast_slab(wks_h[:, ki, :], wkt_h[:, ki, :])
                    for ki in range(NKI)])
    wo_cast = wmax([_cast_slab(wos_h[:, ki * 1024:(ki + 1) * 1024], wot_h[:, ki, :])
                    for ki in range(NKI)])
    wcast_tok = {}
    for e in range(NE):
        toks = []
        for mo in range(NMO):
            toks.append(_cast_slab(
                w1s_h[e, :, mo * 1024:(mo + 1) * 1024],
                (w1_h[e, :, :, mo * 128:(mo + 1) * 128] if e < 8
                 else sw1_h[:, :, mo * 128:(mo + 1) * 128])))
            toks.append(_cast_slab(
                w3s_h[e, :, mo * 1024:(mo + 1) * 1024],
                (w3_h[e, :, :, mo * 128:(mo + 1) * 128] if e < 8
                 else sw3_h[:, :, mo * 128:(mo + 1) * 128])))
        for kh in range(NMO):
            toks.append(_cast_slab(
                w2s_h[e, :, kh * 1024:(kh + 1) * 1024],
                w2_h[e, :, kh, :] if e < 8 else sw2_h[:, kh, :]))
        wcast_tok[e] = wmax(toks)
    E("tensor", "matmul", pb[0][:, 0:8], onesrow_sb, gb_sb, start=True, stop=True,
      wait=wmax(setup_toks), incs=[inc(PE)])
    E("vector", "tensor_copy", gateb_b, pb[0][:, 0:8], wait=[(PE, PE.n)],
      incs=[inc(DV)])
    barrier()

    slab_g = {0: 0, 1: 0}
    h_dv = {0: 0, 1: 0}
    h_ac = {0: 0, 1: 0}
    oe_dv = {0: 0, 1: 0}
    oe_last_pe = 0

    xt_guard = {0: 0, 1: 0}
    for seq in range(BLOC):
        xa_dv = {}
        pt_ac = {0: 0, 1: 0}
        tp_pe_prev = 0
        xnbf = sbuf_s.bitcast(BF16)[:, 0:1024]
        xnT_ac = {}
        vmark = {}
        v_dv = {0: 0, 1: 0}
        gate_ac = {0: 0, 1: 0}
        gate_dv = {0: 0, 1: 0}
        if seq == 0:
            wv_done = wmax(wv0_toks)
        else:
            wv_toks = [QS.dma(wvsb[:, ki * 1024:(ki + 1) * 1024], wvs_h[:, ki, :],
                              wait=([(PE, PE.n)] if ki == 0 else []))
                       for ki in range(NKI)]
            wv_done = wmax(wv_toks)
        for tau in range(NT):
            b = tau % 2
            xload = QX.dma(xt[b], x_h[seq, tau * 128:(tau + 1) * 128, :],
                           wait=([(DV, xt_guard[b])] if xt_guard[b] else []))
            E("scalar", "activation", gbuf, xt[b], AF.Square,
              wait=[xload], incs=[inc(AC)])
            sq_done = AC.n
            E("vector", "reduce_sum", s1c, xt[b], axis=AX.X, wait=[xload])
            E("vector", "reduce_sum", s2c, gbuf, axis=AX.X, wait=[(AC, sq_done)])
            E("vector", "tensor_scalar_mul", stat_m[:, tau:tau + 1], s1c, 1.0 / D,
              incs=[inc(DV)])
            E("vector", "tensor_tensor", m1c, stat_m[:, tau:tau + 1],
              stat_m[:, tau:tau + 1], ALU.mult, wait=[(DV, DV.n)], incs=[inc(DV)])
            E("vector", "scalar_tensor_tensor", m2c, s2c, 1.0 / D, m1c,
              ALU.mult, ALU.subtract, wait=[(DV, DV.n)], incs=[inc(DV)])
            var_done = DV.n
            E("scalar", "activation", m2c, m2c, AF.Ln, bias=LN_EPS,
              wait=[(DV, var_done)])
            E("scalar", "activation", stat_r[:, tau:tau + 1], m2c, AF.Exp,
              scale=-0.5, incs=[inc(AC)])
            rstd_done = AC.n
            E("vector", "tensor_scalar", xnbf, xt[b], stat_m[:, tau:tau + 1],
              stat_r[:, tau:tau + 1], ALU.subtract, ALU.mult,
              wait=[(AC, rstd_done), (PE, tp_pe_prev)], incs=[inc(DV)])
            xa_dv[tau] = DV.n
            xt_guard[b] = DV.n
            xn_done = DV.n
            ptb = pt[tau % 2]
            for ki in range(NKI):
                E("tensor", "transpose", ptb[:, ki * 128:(ki + 1) * 128],
                  xnbf[:, ki * 128:(ki + 1) * 128], idbf_sb,
                  wait=([(DV, xn_done), (AC, pt_ac[tau % 2])] if ki == 0 else []),
                  incs=([inc(PE)] if ki == NKI - 1 else []))
            tp_pe_prev = PE.n
            for ki in range(NKI):
                w = []
                if ki == 0:
                    w = [(PE, tp_pe_prev)]
                    if seq > 0 and tau == 0:
                        w.append((PE, PE.n))
                E("scalar", "copy",
                  xnT[:, ki * S + tau * 128: ki * S + (tau + 1) * 128],
                  ptb[:, ki * 128:(ki + 1) * 128],
                  wait=w, incs=([inc(AC)] if ki == NKI - 1 else []))
            pt_ac[tau % 2] = AC.n
            xnT_ac[tau] = AC.n
        for tau in range(NT):
            b = tau % 2
            bank = pb[2 + tau % 2]
            for ki in range(NKI):
                E("tensor", "matmul", bank[:, 0:8],
                  xnT[:, ki * S + tau * 128: ki * S + (tau + 1) * 128],
                  gate_sb[:, ki * 8:(ki + 1) * 8],
                  start=(ki == 0), stop=(ki == NKI - 1),
                  wait=([(AC, max(xnT_ac[tau], gate_ac[tau % 2])),
                         (DV, gate_dv[tau % 2])] if ki == 0 else []),
                  incs=([inc(PE)] if ki == NKI - 1 else []))
            g_done = PE.n
            E("vector", "reduce_max", m1c, bank[:, 0:8], axis=AX.X, negate=True,
              wait=[(PE, g_done)], incs=[inc(DV)])
            E("scalar", "activation", tiny, bank[:, 0:8], AF.Exp, bias=m1c,
              wait=[(DV, DV.n)], incs=[inc(AC)])
            gate_ac[tau % 2] = AC.n
            for meth, args in [
                ("reduce_sum", (s1c, tiny)),
                ("reciprocal", (s2c, s1c)),
                ("tensor_scalar_mul", (tiny, tiny, s2c)),
                ("tensor_add", (tiny2, tiny, gateb_b)),
                ("reduce_max", (m1c, tiny2)),
                ("tensor_single_scalar", (tiny3, tiny2, m1c, ALU.is_equal)),
                ("scalar_tensor_tensor", (tiny3, tiny3, -1e30, tiny2,
                                          ALU.mult, ALU.add)),
                ("reduce_max", (m2c, tiny3)),
                ("tensor_single_scalar", (tiny2, tiny2, m2c, ALU.is_ge)),
                ("tensor_mul", (rw_sb[:, tau * 8:(tau + 1) * 8], tiny2, tiny)),
                ("tensor_copy", (rwx[:, 0:8], rw_sb[:, tau * 8:(tau + 1) * 8])),
            ]:
                kwx = {"axis": AX.X} if meth.startswith("reduce") else {}
                w = [(DV, DV.n)]
                if meth == "reduce_sum":
                    w.append((AC, AC.n))
                E("vector", meth, *args, wait=w, incs=[inc(DV)], **kwx)
            E("vector", "memset", rwx[:, 9:16], 0.0)
            E("vector", "memset", rwx[:, 8:9], 1.0, incs=[inc(DV)])
            rwx_done = DV.n
            gate_dv[tau % 2] = rwx_done
            ptb2 = pt[1 - tau % 2]
            E("tensor", "transpose", ptb2[0:16, 0:128], rwx, idbf_sb,
              wait=[(DV, rwx_done), (AC, pt_ac[1 - tau % 2])], incs=[inc(PE)])
            E("scalar", "copy", rwT[:, tau * 128:(tau + 1) * 128],
              ptb2[0:9, 0:128], wait=[(PE, PE.n)], incs=[inc(AC)])
            pt_ac[1 - tau % 2] = AC.n
            for ncol in range(2):
                i2 = (tau * 2 + ncol) % 2
                vbank = pb[i2]
                for ki in range(NKI):
                    w = []
                    if ki == 0:
                        w = wv_done + [(DV, v_dv[i2]), (AC, xnT_ac[tau])]
                    E("tensor", "matmul", vbank[:, 0:512],
                      xnT[:, ki * S + tau * 128: ki * S + (tau + 1) * 128],
                      wvsb[:, ki * 1024 + ncol * 512: ki * 1024 + (ncol + 1) * 512],
                      start=(ki == 0), stop=False, wait=w)
                E("tensor", "matmul", vbank[:, 0:512], onesrow_sb,
                  bvr_st[:, ncol * 512:(ncol + 1) * 512],
                  start=False, stop=True, incs=[inc(PE)])
                w = [(PE, PE.n)]
                if seq > 0 and tau == 0 and ncol == 0:
                    w.append((AC, AC.n))
                E("vector", "tensor_copy",
                  v_nat[:, tau * 1024 + ncol * 512: tau * 1024 + (ncol + 1) * 512],
                  vbank[:, 0:512], wait=w, incs=[inc(DV)])
                v_dv[i2] = DV.n
                vmark[tau] = DV.n

        proj_ac = {0: 0, 1: 0}
        dst_dv = {0: 0, 1: 0}
        z_ac = {0: 0, 1: 0}
        pv_pe = {}
        sq_ac = {}
        pair_pe = {0: 0, 1: 0}
        QKBUF = {0: (q1b, sqb, t1buf, t2buf, rstd_f, rstd_b),
                 1: (q1b_k, sqb_k, t1buf_k, t2buf_k, rstd_f_k, rstd_b_k)}
        SBANK = {(0, 0): pb[2], (0, 1): pb[2], (1, 0): pb[3], (1, 1): pb[3]}
        wqk_tok = {}
        wqsl = wqs_h.rearrange("p k (pp o) -> p k pp o", pp=NPAIR)
        wksl = wks_h.rearrange("p k (pp o) -> p k pp o", pp=NPAIR)
        for pair in range(2):
            w0 = [(PE, PE.n)] if pair == 0 else []
            if seq == 0 and pair == 0:
                w0 = w0 + wq_cast + wk_cast
            wqk_tok[pair] = wmax([
                QS.dma(wqp2[pair % 2], wqsl[:, :, pair, :], wait=w0),
                QS.dma(wkp2[pair % 2], wksl[:, :, pair, :])])
        for pair in range(NPAIR):
            wqk_done = wqk_tok[pair]
            wqp, wkp = wqp2[pair % 2], wkp2[pair % 2]
            kadd = {0: 0, 1: 0}
            for c in range(4):
                pdone = {}
                for qk in range(2):
                    wsel = wqp if qk == 0 else wkp
                    for ki in range(NKI):
                        w = []
                        if ki == 0:
                            w = wqk_done + [(AC, max(proj_ac[qk], xnT_ac[c * 4 + 3],
                                                     sq_ac.get((qk, 1), 0))),
                                            (DV, max(dst_dv[qk],
                                                     vmark[NT - 1] if (pair, c) == (0, 0) else 0))]
                        E("tensor", "matmul", pb[qk][:, 0:512],
                          wsel[:, ki * 128:(ki + 1) * 128],
                          xnT[:, ki * S + c * 512: ki * S + (c + 1) * 512],
                          start=(ki == 0), stop=(ki == NKI - 1), wait=w,
                          incs=([inc(PE)] if ki == NKI - 1 else []))
                    pdone[qk] = PE.n
                if c == 3 and pair + 2 < NPAIR:
                    p2 = pair + 2
                    wqk_tok[p2] = wmax([
                        QS.dma(wqp2[p2 % 2], wqsl[:, :, p2, :],
                               wait=[(PE, pdone[1])]),
                        QS.dma(wkp2[p2 % 2], wksl[:, :, p2, :])])
                st = {}
                for qk in range(2):
                    bsel = bqt_sb if qk == 0 else bkt_sb
                    _q1 = QKBUF[qk][0]
                    E("scalar", "activation", _q1, pb[qk][:, 0:512], AF.Identity,
                      bias=bsel[:, pair:pair + 1],
                      wait=[(PE, pdone[qk]), (DV, dst_dv[qk])])
                for qk in range(2):
                    bsel = bqt_sb if qk == 0 else bkt_sb
                    _sq = QKBUF[qk][1]
                    E("scalar", "activation", _sq, pb[qk][:, 0:512], AF.Square,
                      bias=bsel[:, pair:pair + 1], incs=[inc(AC)])
                    proj_ac[qk] = AC.n
                for qk in range(2):
                    E("tensor", "matmul", pb[qk][0:2, 0:512], ones2_sb,
                      QKBUF[qk][1], start=True, stop=True,
                      wait=[(AC, proj_ac[qk])], incs=[inc(PE)])
                    st[qk] = PE.n
                for qk in range(2):
                    E("scalar", "activation", QKBUF[qk][4], pb[qk][0:2, 0:512],
                      AF.Ln, scale=1.0 / DH, bias=RMS_EPS, wait=[(PE, st[qk])])
                for qk in range(2):
                    E("scalar", "activation", QKBUF[qk][4], QKBUF[qk][4], AF.Exp,
                      scale=-0.5, bias=(-math.log(DH) if qk == 0 else 0.0),
                      incs=[inc(AC)])
                    st[qk] = AC.n
                for qk in range(2):
                    E("vector", "tensor_copy", QKBUF[qk][5], QKBUF[qk][4],
                      wait=[(AC, st[qk])], incs=[inc(DV)])
                    st[qk] = DV.n
                for qk in range(2):
                    E("tensor", "matmul", pb[qk][:, 0:512], sel2_sb, QKBUF[qk][5],
                      start=True, stop=True, wait=[(DV, st[qk])], incs=[inc(PE)])
                    st[qk] = PE.n
                for qk in range(2):
                    E("vector", "tensor_tensor", QKBUF[qk][1], QKBUF[qk][0],
                      pb[qk][:, 0:512], ALU.mult, wait=[(PE, st[qk])],
                      incs=[inc(DV)])
                    st[qk] = DV.n
                for qk in range(2):
                    E("tensor", "matmul", pb[qk][:, 0:512], prot_sb, QKBUF[qk][1],
                      start=True, stop=True, wait=[(DV, st[qk])], incs=[inc(PE)])
                    st[qk] = PE.n
                    if qk == 1 and c == 3:
                        pair_pe[pair % 2] = PE.n
                for qk in range(2):
                    E("vector", "tensor_tensor", QKBUF[qk][3], pb[qk][:, 0:512],
                      sw[:, c * 512:(c + 1) * 512], ALU.mult, wait=[(PE, st[qk])])
                for qk in range(2):
                    E("vector", "tensor_tensor", QKBUF[qk][2], QKBUF[qk][1],
                      cw[:, c * 512:(c + 1) * 512], ALU.mult)
                for qk in range(2):
                    dst = qTp if qk == 0 else kTp
                    E("vector", "tensor_add", dst[:, c * 512:(c + 1) * 512],
                      QKBUF[qk][2], QKBUF[qk][3], incs=[inc(DV)])
                    dst_dv[qk] = DV.n
                    kadd[qk] = DV.n
                qc = c
                nkt = qc * 4 + 4
                for kt in range(nkt):
                    patt = {}
                    for hh in range(2):
                        sbank = SBANK[(hh, kt % 2)]
                        pbf = pbuf[2 * hh + (kt % 2)]
                        w = [(AC, sq_ac.get((hh, kt % 2), 0))]
                        if kt <= 1:
                            w.append((DV, max(kadd[0], kadd[1])))
                        E("tensor", "matmul", sbank[:, 0:512],
                          kTp[64 * hh:64 * hh + 64, kt * 128:(kt + 1) * 128],
                          qTp[64 * hh:64 * hh + 64, qc * 512:(qc + 1) * 512],
                          start=True, stop=True, tile_position=(64 * hh, 0),
                          skip_group_check=True, wait=w, incs=[inc(PE)])
                        s_done = PE.n
                        wsq = [(PE, s_done), (PE, pv_pe.get((hh, kt % 2), 0))]
                        E("scalar", "activation", pbf, sbank[:, 0:512],
                          AF.Square, wait=wsq, incs=[inc(AC)])
                        sq_ac[(hh, kt % 2)] = AC.n
                        if kt >= qc * 4:
                            off = (kt - qc * 4) * 128
                            if off > 0:
                                E("vector", "memset", pbf[:, 0:off], 0.0,
                                  wait=[(AC, AC.n)])
                            E("vector", "tensor_tensor", pbf[:, off:off + 128],
                              pbf[:, off:off + 128], tri_sb, ALU.mult,
                              wait=([(AC, AC.n)] if off == 0 else []),
                              incs=[inc(DV)])
                            patt[hh] = (DV, DV.n)
                        else:
                            patt[hh] = (AC, AC.n)
                    for hh in range(2):
                        zbank = pb[4 + hh]
                        pbf = pbuf[2 * hh + (kt % 2)]
                        w = [patt[hh], (DV, vmark[kt])]
                        if kt == 0:
                            w.append((AC, z_ac[hh]))
                        E("tensor", "matmul", zbank[0:64, 0:512],
                          v_nat[:, kt * 1024 + (2 * pair + hh) * 64:
                                kt * 1024 + (2 * pair + hh) * 64 + 64],
                          pbf, start=(kt == 0), stop=(kt == nkt - 1),
                          skip_group_check=True, wait=w, incs=[inc(PE)])
                        pv_pe[(hh, kt % 2)] = PE.n
                for hh in range(2):
                    E("scalar", "copy",
                      zT[64 * hh:64 * hh + 64,
                         pair * S + qc * 512: pair * S + (qc + 1) * 512],
                      pb[4 + hh][0:64, 0:512],
                      wait=[(PE, pv_pe[(hh, (nkt - 1) % 2)])], incs=[inc(AC)])
                    z_ac[hh] = AC.n
        barrier()

        for half in range(2):
            t0 = half * HALF
            yinit_dv = {0: 0, 1: 0}
            for tl in range(NTL):
                tau = half * NTL + tl
                for ncol in range(2):
                    i2 = (tl * 2 + ncol) % 2
                    bank = pb[4 + i2]
                    E("tensor", "matmul", bank[:, 0:512],
                      rwT[:, tau * 128:(tau + 1) * 128],
                      bp_sb[:, ncol * 512:(ncol + 1) * 512],
                      start=True, stop=True,
                      wait=([(DV, yinit_dv[i2])] if yinit_dv[i2] else []),
                      incs=[inc(PE)])
                    E("vector", "tensor_copy",
                      y_acc[:, tl * 1024 + ncol * 512: tl * 1024 + (ncol + 1) * 512],
                      bank[:, 0:512], wait=[(PE, PE.n)], incs=[inc(DV)])
                    yinit_dv[i2] = DV.n
            oe_dv[0] = oe_dv[1] = DV.n
            for e in range(NE):
                silu_first = True
                for mo in range(NMO):
                    sb_i = mo % 2
                    w = [(PE, slab_g[sb_i])]
                    if seq == 0 and half == 0 and mo == 0:
                        w = w + wcast_tok[e]
                    q1q = QS if (seq == 0 and half == 0) else QW
                    t1_ = q1q.dma(w1m[sb_i], w1s_h[e, :, mo * 1024:(mo + 1) * 1024],
                                  wait=w)
                    t3_ = QS.dma(w3m[sb_i], w3s_h[e, :, mo * 1024:(mo + 1) * 1024],
                                 wait=w)
                    slab_dma = wmax([t1_, t3_])
                    for ncol in range(2):
                        i2 = (mo * 2 + ncol) % 2
                        g1bank, g3bank = pb[0 + i2], pb[2 + i2]
                        for ki in range(NKI):
                            w = []
                            if ki == 0:
                                w = slab_dma + [(DV, h_dv[i2])]
                            E("tensor", "matmul", g1bank[:, 0:512],
                              w1m[sb_i][:, ki * 128:(ki + 1) * 128],
                              xnT[:, ki * S + t0 + ncol * 512:
                                  ki * S + t0 + (ncol + 1) * 512],
                              start=(ki == 0), stop=(ki == NKI - 1), wait=w)
                        for ki in range(NKI):
                            E("tensor", "matmul", g3bank[:, 0:512],
                              w3m[sb_i][:, ki * 128:(ki + 1) * 128],
                              xnT[:, ki * S + t0 + ncol * 512:
                                  ki * S + t0 + (ncol + 1) * 512],
                              start=(ki == 0), stop=(ki == NKI - 1),
                              incs=([inc(PE)] if ki == NKI - 1 else []))
                        gg_done = PE.n
                        if ncol == 1:
                            slab_g[sb_i] = gg_done
                        E("vector", "tensor_scalar_add", t3b, g3bank[:, 0:512],
                          b3t_sb[:, e * NMO + mo: e * NMO + mo + 1],
                          wait=[(PE, gg_done), (AC, h_ac[i2])], incs=[inc(DV)])
                        E("vector", "scalar_tensor_tensor", prodb[i2],
                          g1bank[:, 0:512],
                          b1t_sb[:, e * NMO + mo: e * NMO + mo + 1], t3b,
                          ALU.add, ALU.mult, wait=[(DV, DV.n)], incs=[inc(DV)])
                        h_dv[i2] = DV.n
                        wsilu = [(DV, DV.n)]
                        if silu_first:
                            wsilu.append((PE, oe_last_pe))
                            wsilu += QS.drain_tokens()
                            silu_first = False
                        E("scalar", "activation",
                          hT[:, mo * 1024 + ncol * 512: mo * 1024 + (ncol + 1) * 512],
                          prodb[i2], AF.Silu, wait=wsilu, incs=[inc(AC)])
                        h_ac[i2] = AC.n
                h_done_ac = AC.n
                w2_toks = []
                for hi in range(2):
                    off = 0 if hi == 0 else 6 * 1024
                    nel = 6 * 1024 if hi == 0 else 5 * 1024
                    w = []
                    if hi == 0:
                        w = [(PE, oe_last_pe)]
                        if seq == 0 and half == 0:
                            w = w + wcast_tok[e]
                    w2_toks.append(QS.dma(w2sb[:, off:off + nel],
                                          w2s_h[e, :, off:off + nel], wait=w))
                w2_dma = wmax(w2_toks)
                for tl in range(NTL):
                    tau = half * NTL + tl
                    for ncol in range(2):
                        i2 = (tl * 2 + ncol) % 2
                        bank = pb[4 + i2]
                        for kh in range(NMO):
                            w = []
                            if kh == 0:
                                w = w2_dma + [(AC, h_done_ac), (DV, oe_dv[i2])]
                            E("tensor", "matmul", bank[:, 0:512],
                              hT[:, kh * 1024 + tl * 128: kh * 1024 + (tl + 1) * 128],
                              w2sb[:, kh * 1024 + ncol * 512:
                                   kh * 1024 + (ncol + 1) * 512],
                              start=(kh == 0), stop=(kh == NMO - 1), wait=w,
                              incs=([inc(PE)] if kh == NMO - 1 else []))
                        oe_last_pe = PE.n
                        ysl = y_acc[:, tl * 1024 + ncol * 512:
                                    tl * 1024 + (ncol + 1) * 512]
                        if e < 8:
                            E("vector", "scalar_tensor_tensor", ysl, bank[:, 0:512],
                              rw_sb[:, tau * 8 + e: tau * 8 + e + 1], ysl,
                              ALU.mult, ALU.add, wait=[(PE, oe_last_pe)],
                              incs=[inc(DV)])
                        else:
                            E("vector", "tensor_add", ysl, ysl, bank[:, 0:512],
                              wait=[(PE, oe_last_pe)], incs=[inc(DV)])
                        oe_dv[i2] = DV.n
            barrier()
            wo_toks = [QW.dma(wosb, wos_h,
                              wait=[(PE, oe_last_pe)]
                              + (wo_cast if seq == 0 and half == 0 else []))]
            wo_dma = wmax(wo_toks)
            fin_dv = {0: 0, 1: 0}
            fin_st = {0: [], 1: []}
            fin_nc = {0: 0, 1: 0}
            last_fin_dv = 0
            for tl in range(NTL):
                tau = half * NTL + tl
                b = tl % 2
                xload = QX.dma(xt[b], x_h[seq, tau * 128:(tau + 1) * 128, :],
                               wait=([(DV, xt_guard[b])] if xt_guard[b] else []))
                E("scalar", "activation", gbuf, y_acc[:, tl * 1024:(tl + 1) * 1024],
                  AF.Silu, wait=[(DV, last_fin_dv)], incs=[inc(AC)])
                g_done = AC.n
                swo_done = {}
                for ncol in range(2):
                    i2 = (tl * 2 + ncol) % 2
                    bank = pb[4 + i2]
                    for pp in range(NKI):
                        w = []
                        if pp == 0:
                            w = wo_dma + [(DV, fin_nc[i2] or oe_dv[i2])]
                        E("tensor", "matmul", bank[:, 0:512],
                          zT[:, pp * S + tau * 128: pp * S + (tau + 1) * 128],
                          wosb[:, pp * 1024 + ncol * 512: pp * 1024 + (ncol + 1) * 512],
                          start=(pp == 0), stop=(pp == NKI - 1), wait=w,
                          incs=([inc(PE)] if pp == NKI - 1 else []))
                    swo_done[ncol] = PE.n
                for ncol in range(2):
                    E("vector", "tensor_scalar",
                      sbuf_s[:, ncol * 512:(ncol + 1) * 512],
                      xt[b][:, ncol * 512:(ncol + 1) * 512],
                      stat_m[:, tau:tau + 1], stat_r[:, tau:tau + 1],
                      ALU.subtract, ALU.mult, wait=([xload] if ncol == 0 else []))
                for ncol in range(2):
                    E("vector", "tensor_add", sbuf_s[:, ncol * 512:(ncol + 1) * 512],
                      sbuf_s[:, ncol * 512:(ncol + 1) * 512],
                      pb[4 + (tl * 2 + ncol) % 2][:, 0:512],
                      wait=[(PE, swo_done[ncol])],
                      incs=([inc(DV)] if ncol == 1 else []))
                fin_nc[tl * 2 % 2] = DV.n
                fin_nc[(tl * 2 + 1) % 2] = DV.n
                for ncol in range(2):
                    E("vector", "tensor_mul", outb[b][:, ncol * 512:(ncol + 1) * 512],
                      sbuf_s[:, ncol * 512:(ncol + 1) * 512],
                      gbuf[:, ncol * 512:(ncol + 1) * 512],
                      wait=([(AC, g_done)] + fin_st[b] if ncol == 0 else []))
                for ncol in range(2):
                    E("vector", "tensor_add", outb[b][:, ncol * 512:(ncol + 1) * 512],
                      outb[b][:, ncol * 512:(ncol + 1) * 512],
                      xt[b][:, ncol * 512:(ncol + 1) * 512],
                      incs=([inc(DV)] if ncol == 1 else []))
                fin_dv[b] = DV.n
                xt_guard[b] = DV.n
                last_fin_dv = DV.n
                fin_st[b] = [QO.dma(out_h[seq, tau * 128:(tau + 1) * 128, :],
                                    outb[b], wait=[(DV, fin_dv[b])])]
            barrier()

    if DEBUG:
        barrier()
        for nm, buf in [("d_cw", cw), ("d_sw", sw), ("d_statm", stat_m),
                        ("d_statr", stat_r), ("d_xnT", xnT), ("d_zT", zT),
                        ("d_rw", rw_sb), ("d_rwT", rwT), ("d_y", y_acc),
                        ("d_gate", gate_sb)]:
            QO.dma(dbg[nm], buf)
    E("sync", "nop", wait=QO.drain_tokens() + QS.drain_tokens())

    with nc.Block() as block:
        block.sync(lambda e: p.replay(e, "sync"))
        block.gpsimd(lambda e: p.replay(e, "gpsimd"))
        block.tensor(lambda e: p.replay(e, "tensor"))
        block.vector(lambda e: p.replay(e, "vector"))
        block.scalar(lambda e: p.replay(e, "scalar"))
    return nc


def _layout_inputs(i, **inp):
    f = lambda a: np.asarray(a, np.float32)
    nw = f(inp["L0_attn_norm_w"])
    nws = np.ascontiguousarray(nw.reshape(32, 2)[:, ::-1].reshape(64))
    c = np.ascontiguousarray
    wslab = lambda w: c(f(w).T.reshape(NKI, 128, D).transpose(1, 0, 2))
    return {
        "x": c(f(inp["L0_attn_x"])[2 * i:2 * i + 2]),
        "wqt": wslab(inp["L0_attn_Wq"]),
        "wkt": wslab(inp["L0_attn_Wk"]),
        "wvt": wslab(inp["L0_attn_Wv"]),
        "wot": wslab(inp["L0_attn_Wo"]),
        "bqt": c(f(inp["L0_attn_bq"]).reshape(8, 128).T),
        "bkt": c(f(inp["L0_attn_bk"]).reshape(8, 128).T),
        "bvr": c(f(inp["L0_attn_bv"]).reshape(1, D)),
        "nw": c(np.tile(nw, 2).reshape(128, 1)),
        "nws": c(np.tile(nws, 2).reshape(128, 1)),
        "gwt": c(f(inp["L0_ffn_gate_w"]).T.reshape(NKI, 128, 8).transpose(1, 0, 2).reshape(128, NKI * 8)),
        "gb": c(f(inp["L0_ffn_gate_b"]).reshape(1, 8)),
        "b1t": c(np.concatenate(
            [f(inp["L0_ffn_b1"]).reshape(8, NMO, 128).transpose(0, 2, 1),
             f(inp["L0_ffn_sb1"]).reshape(NMO, 128).T[None]], 0)
            .transpose(1, 0, 2).reshape(128, NE * NMO)),
        "b3t": c(np.concatenate(
            [f(inp["L0_ffn_b3"]).reshape(8, NMO, 128).transpose(0, 2, 1),
             f(inp["L0_ffn_sb3"]).reshape(NMO, 128).T[None]], 0)
            .transpose(1, 0, 2).reshape(128, NE * NMO)),
        "bp": c(np.concatenate(
            [f(inp["L0_ffn_b2"]), f(inp["L0_ffn_sb2"]).reshape(1, D)], 0)),
        "w1": c(f(inp["L0_ffn_w1"]).reshape(8, NKI, 128, H).transpose(0, 2, 1, 3)),
        "w3": c(f(inp["L0_ffn_w3"]).reshape(8, NKI, 128, H).transpose(0, 2, 1, 3)),
        "w2": c(f(inp["L0_ffn_w2"]).reshape(8, NMO, 128, D).transpose(0, 2, 1, 3)),
        "sw1": c(f(inp["L0_ffn_sw1"]).reshape(NKI, 128, H).transpose(1, 0, 2)),
        "sw3": c(f(inp["L0_ffn_sw3"]).reshape(NKI, 128, H).transpose(1, 0, 2)),
        "sw2": c(f(inp["L0_ffn_sw2"]).reshape(NMO, 128, D).transpose(1, 0, 2)),
    }


_NC_CACHE = None


def kernel(**inputs):
    global _NC_CACHE
    if _NC_CACHE is None:
        _NC_CACHE = build_nc()
    in_maps = [_layout_inputs(i, **inputs) for i in range(NCORE)]
    res = run_bass_kernel_spmd(_NC_CACHE, in_maps, core_ids=list(range(NCORE)))
    if DEBUG:
        import pickle
        with open("/root/problem/_dbg_dumps.pkl", "wb") as fh:
            pickle.dump([{k: np.asarray(v) for k, v in r.items()} for r in res.results], fh)
    return np.concatenate([res.results[i]["out"] for i in range(NCORE)], axis=0)


if __name__ == "__main__":
    build_nc()
    print("built ok")
